# version 02
# baseline (speedup 1.0000x reference)
import math
import numpy as np
import ml_dtypes
import concourse.bass as bass
import concourse.mybir as mybir
from concourse.bass_utils import run_bass_kernel_spmd
from contextlib import ExitStack

F32 = mybir.dt.float32
BF16 = mybir.dt.bfloat16
I32 = mybir.dt.int32
ALU = mybir.AluOpType
ACT = mybir.ActivationFunctionType
AX = mybir.AxisListType

L = 4096
E = 1024
NT = L // 128
NC = E // 128
H = 16
D = 64
MF = 256
I_FFN = 4096

STAGES = ("l0attn", "l0ffn", "l1attn", "l1ffn")
DEV_TAP = False
RACE_CHECK = False
L1A_LIMIT = None
L1A_P1PARTS = 3
L1A_PREP = 7
L1A_SUB = 9
L1A_EVENONLY = False

TWO_PI = np.float64(2 * np.pi)
CW1 = float(np.float32(TWO_PI))
CW2 = float(np.float32(TWO_PI - np.float64(np.float32(TWO_PI))))
CW3 = float(TWO_PI - np.float64(np.float32(CW1)) - np.float64(np.float32(CW2)))
INV2PI = float(np.float32(1.0 / TWO_PI))
PIH_HI = float(np.float32(np.pi / 2))
PIH_LO = float(np.float64(np.pi / 2) - np.float64(np.float32(np.pi / 2)))


class Sem:
    def __init__(self, stack, nc, name):
        self.h = stack.enter_context(nc.semaphore(name))
        self.n = 0


class Prog:
    ENGINES = ("sync", "gpsimd", "vector", "scalar", "tensor")

    def __init__(self):
        self.steps = {e: [] for e in self.ENGINES}

    def on(self, eng, fn):
        self.steps[eng].append(fn)

    def wait(self, eng, sem, val=None):
        v = sem.n if val is None else val
        self.on(eng, lambda e, s=sem.h, v=v: e.wait_ge(s, v))

    def op(self, eng, emit, inc=None, n=1):
        if inc is not None:
            inc.n += n
            tgt = (inc.h, n)
        else:
            tgt = None

        def f(e, emit=emit, tgt=tgt):
            i = emit(e)
            if tgt is not None:
                i.then_inc(*tgt)
        self.on(eng, f)


class Ctx:
    pass


def build_nc():
    nc = bass.Bass("TRN2", target_bir_lowering=False,
                   detect_race_conditions=RACE_CHECK)
    st = ExitStack()
    p = Prog()
    c = Ctx()
    c.nc, c.st, c.p = nc, st, p

    c.x_in = nc.dram_tensor("x", [L, E], F32, kind="ExternalInput").ap()
    c.out_d = nc.dram_tensor("out", [L, E], F32, kind="ExternalOutput").ap()
    if DEV_TAP:
        c.dbg_d = nc.dram_tensor("dbg", [L, E], F32, kind="ExternalOutput").ap()
        c.dbg2_d = nc.dram_tensor("dbg2", [L, 16], F32, kind="ExternalOutput").ap()

    def din(name, shape):
        return nc.dram_tensor(name, list(shape), F32, kind="ExternalInput").ap()

    c.w_qkv0T = din("w_qkv0T", [E, 3 * E])
    c.w_out0T = din("w_out0T", [E, E])
    c.sorf_signs_t = din("sorf_signs_t", [64, 192])
    c.sorf_scales_t = din("sorf_scales_t", [64, 64])
    c.sorf_scales_r = din("sorf_scales_r", [64, 64])
    c.th0 = din("theta0", [E, E])
    c.ph0 = din("phi0", [E, E])
    c.thn0 = din("theta0n", [E, E])
    c.phn0 = din("phi0n", [E, E])
    c.w_qkv1T = din("w_qkv1T", [E, 3 * E])
    c.b_qkv1 = din("b_qkv1", [1, 3 * E])
    c.w_latT = din("w_latT", [D, D])
    c.b_lat = din("b_lat", [1, D])
    c.w_proj1T = din("w_proj1T", [E, E])
    c.b_proj1 = din("b_proj1", [1, E])
    c.w_gateT = din("w_gateT", [E, I_FFN])
    c.w_upT = din("w_upT", [E, I_FFN])
    c.w_downT = din("w_downT", [I_FFN, E])
    c.posc = din("posc", [128, NT])
    c.had_c = din("had_c", [64, 64])
    c.had_p = din("had_p", [64, 64])
    c.grid_c = din("grid_c", [1, 32])
    c.masks_c = din("masks_c", [128, 144])
    c.b_qkv1c = din("b_qkv1c", [128, 24])
    c.b_lat2 = din("b_lat2", [128, 1])
    c.ten_thousand = din("tenk_c", [1, 1])

    c.qr_d = nc.dram_tensor("qr_d", [128, NC * L], BF16, kind="Internal").ap()
    c.kr_d = nc.dram_tensor("kr_d", [128, NC * L], BF16, kind="Internal").ap()
    c.qf_d = nc.dram_tensor("qf_d", [128, NC * L], BF16, kind="Internal").ap()
    c.kl_d = nc.dram_tensor("kl_d", [128, NC * L], BF16, kind="Internal").ap()
    c.vl_d = nc.dram_tensor("vl_d", [L, E], BF16, kind="Internal").ap()
    c.qgate_d = nc.dram_tensor("qgate_d", [E, I_FFN], BF16, kind="Internal").ap()
    c.qup_d = nc.dram_tensor("qup_d", [E, I_FFN], BF16, kind="Internal").ap()
    c.qdown_d = nc.dram_tensor("qdown_d", [I_FFN, E], BF16, kind="Internal").ap()

    def sb(name, shape, dt=F32):
        return st.enter_context(nc.sbuf_tensor(name, list(shape), dt))

    c.X_TM = sb("X_TM", [128, NT * E], BF16)
    c.X_FM = sb("X_FM", [128, NC * L], BF16)
    c.C_all = sb("C_all", [128, NT * 32], BF16)
    c.S_all = sb("S_all", [128, NT * 32], BF16)
    c.POS = sb("POS", [128, NT], F32)
    c.FREQB = sb("FREQB", [128, 32], F32)
    c.IDM = sb("IDM", [128, 128], BF16)
    c.ONES1 = sb("ONES1", [1, 512], BF16)
    c.NEGH = sb("NEGH", [128, 256], BF16)
    c.WS = sb("WS", [128, 8 * MF], BF16)
    c.ROW = sb("ROW", [128, 128], F32)
    c.ONESF_COL_T = sb("ONESF_COL_T", [128, 1], F32)
    c.ONESF_ROW_T = sb("ONESF_ROW_T", [1, 128], F32)
    c.ONESF_COL = c.ONESF_COL_T[:]
    c.ONESF_ROW = c.ONESF_ROW_T[:]
    c.ABF = sb("ABF", [128, 20480], BF16)
    c.ONESB_ROW = c.ABF[0:1, 18560:18688]
    c.BPROJR = c.ABF[0:1, 18688:19712]
    c.MASKB = c.ABF[:, 19712:19856]
    c.AF32 = sb("AF32", [128, 6656], F32)
    c.AI16 = sb("AI16", [128, 1024], mybir.dt.int16)
    c.OUTST = c.AF32[:, 1024:2048]

    c.PSA = st.enter_context(nc.psum_tensor("psA", [128, 2048], F32))
    c.PSB2 = st.enter_context(nc.psum_tensor("psB", [128, 2048], F32))
    c.PS = ([c.PSA[:, i * 512:(i + 1) * 512] for i in range(4)]
            + [c.PSB2[:, i * 512:(i + 1) * 512] for i in range(4)])
    c.PS0_11 = c.PSA[0:1, 0:1]
    c.PS0_C1 = c.PSA[:, 0:1]

    c.s_w = Sem(st, nc, "s_w")
    c.s_ldx = Sem(st, nc, "s_ldx")
    c.s_ws = Sem(st, nc, "s_ws")
    c.s_wp = [Sem(st, nc, "s_wp0"), Sem(st, nc, "s_wp1")]
    c.s_trp = [Sem(st, nc, "s_trp0"), Sem(st, nc, "s_trp1")]
    c.s_tr = Sem(st, nc, "s_tr")
    c.s_v = Sem(st, nc, "s_v")
    c.s_sc = Sem(st, nc, "s_sc")
    c.s_t = Sem(st, nc, "s_t")
    c.s_st = Sem(st, nc, "s_st")
    c.s_w2 = Sem(st, nc, "s_w2")
    c.s_ws2 = Sem(st, nc, "s_ws2")

    build_program(c)

    with nc.Block() as block:
        @block.sync
        def _(e):
            for f in p.steps["sync"]:
                f(e)

        @block.gpsimd
        def _(e):
            for f in p.steps["gpsimd"]:
                f(e)

        @block.vector
        def _(e):
            for f in p.steps["vector"]:
                f(e)

        @block.scalar
        def _(e):
            for f in p.steps["scalar"]:
                f(e)

        @block.tensor
        def _(e):
            for f in p.steps["tensor"]:
                f(e)

    return nc


def build_program(c):
    p = c.p
    p.op("vector", lambda e: e.memset(c.ONESF_COL, 1.0), inc=c.s_v)
    p.op("vector", lambda e: e.memset(c.ONESF_ROW, 1.0), inc=c.s_v)

    for t in range(NT):
        p.op("gpsimd",
             lambda e, t=t: e.dma_start(
                 out=c.X_TM[:, t * E:(t + 1) * E],
                 in_=c.x_in[t * 128:(t + 1) * 128, :]),
             inc=c.s_ldx, n=16)

    p.wait("sync", c.s_ldx)
    for t in range(NT):
        for cc in range(NC):
            p.op("sync",
                 lambda e, t=t, cc=cc: e.dma_start(
                     out=c.X_FM[:, cc * L + t * 128:cc * L + (t + 1) * 128],
                     in_=c.X_TM[:, t * E + cc * 128:t * E + (cc + 1) * 128],
                     transpose=True),
                 inc=c.s_tr, n=16)

    for eng in ("vector", "scalar", "tensor", "gpsimd"):
        p.wait(eng, c.s_ldx)
        p.wait(eng, c.s_tr)

    stages = list(STAGES)
    for si, stname in enumerate(stages):
        last = (si == len(stages) - 1)
        dict(l0ffn=emit_l0ffn, l0attn=emit_l0attn,
             l1attn=emit_l1attn, l1ffn=emit_l1ffn)[stname](c, last)

    p.wait("sync", c.s_st)


def emit_ln(c, t, ps_lo, ps_hi, last, scale_ap=None):
    p = c.p
    r = c.AF32[:, 0:1024]
    sq = c.AF32[:, 1024:2048]
    sums, ssq, mean, var, rstd, nbias, m2 = (c.ROW[:, i:i + 1] for i in range(7))

    op0 = ALU.add if scale_ap is None else ALU.mult
    sc0 = 0.0 if scale_ap is None else scale_ap
    p.wait("vector", c.s_sc)
    p.wait("vector", c.s_st)
    p.op("vector", lambda e: e.scalar_tensor_tensor(
        r[:, 0:512], ps_lo[:], sc0, c.X_TM[:, t * E:t * E + 512],
        op0, ALU.add, accum_out=sums), inc=c.s_v)
    p.op("vector", lambda e: e.scalar_tensor_tensor(
        r[:, 512:1024], ps_hi[:], sc0, c.X_TM[:, t * E + 512:(t + 1) * E],
        op0, ALU.add, accum_out=ssq), inc=c.s_v)
    p.wait("vector", c.s_v)
    p.op("vector", lambda e: e.tensor_add(sums, sums, ssq), inc=c.s_v)
    p.wait("scalar", c.s_v)
    if last:
        p.wait("scalar", c.s_st)
    p.op("scalar", lambda e: e.activation(sq, r, ACT.Square, accum_out=ssq),
         inc=c.s_sc)
    p.wait("vector", c.s_sc)
    p.op("vector", lambda e: e.tensor_scalar(mean, sums, 1.0 / E, None, ALU.mult),
         inc=c.s_v)
    p.op("vector", lambda e: e.tensor_scalar(var, ssq, 1.0 / E, 1e-5, ALU.mult,
                                             ALU.add), inc=c.s_v)
    p.wait("vector", c.s_v)
    p.op("vector", lambda e: e.tensor_mul(m2, mean, mean), inc=c.s_v)
    p.wait("vector", c.s_v)
    p.op("vector", lambda e: e.tensor_sub(var, var, m2), inc=c.s_v)
    p.wait("vector", c.s_v)
    p.op("vector", lambda e: e.reciprocal(var, var), inc=c.s_v)
    p.wait("scalar", c.s_v)
    p.op("scalar", lambda e: e.activation(rstd, var, ACT.Sqrt), inc=c.s_sc)
    p.wait("vector", c.s_sc)
    p.op("vector", lambda e: e.scalar_tensor_tensor(
        nbias, mean, -1.0, rstd, ALU.mult, ALU.mult), inc=c.s_v)
    p.wait("scalar", c.s_v)
    if last:
        p.wait("scalar", c.s_st)
        p.op("scalar", lambda e: e.activation(
            c.OUTST, r, ACT.Identity, bias=nbias, scale=rstd), inc=c.s_sc)
        p.wait("sync", c.s_sc)
        p.op("sync", lambda e: e.dma_start(
            out=c.out_d[t * 128:(t + 1) * 128, :], in_=c.OUTST),
            inc=c.s_st, n=16)
    else:
        p.op("scalar", lambda e: e.activation(
            c.X_TM[:, t * E:(t + 1) * E], r, ACT.Identity, bias=nbias, scale=rstd),
            inc=c.s_sc)
        p.wait("sync", c.s_sc)
        for cc in range(NC):
            p.op("sync", lambda e, cc=cc: e.dma_start(
                out=c.X_FM[:, cc * L + t * 128:cc * L + (t + 1) * 128],
                in_=c.X_TM[:, t * E + cc * 128:t * E + (cc + 1) * 128],
                transpose=True), inc=c.s_tr, n=16)


def emit_l0ffn(c, last):
    p = c.p
    Cw = c.ABF[:, 0:NC * E]
    Sw = c.ABF[:, NC * E:2 * NC * E]
    tbuf = c.AF32[:, 2048:3072]
    nbuf = c.AF32[:, 3072:4096]
    re2 = c.AF32[:, 2048:3072]
    im2 = c.AF32[:, 3072:4096]
    mag = c.AF32[:, 4096:5120]

    for cc in range(NC):
        for (wsrc, nsrc, dst, shift) in ((c.th0, c.thn0, Cw, True),
                                         (c.ph0, c.phn0, Sw, False)):
            p.wait("gpsimd", c.s_sc)
            p.op("gpsimd", lambda e, w=wsrc, cc=cc: e.dma_start(
                out=tbuf, in_=w[cc * 128:(cc + 1) * 128, :]), inc=c.s_w, n=16)
            p.op("gpsimd", lambda e, w=nsrc, cc=cc: e.dma_start(
                out=nbuf, in_=w[cc * 128:(cc + 1) * 128, :]), inc=c.s_w, n=16)
            p.wait("vector", c.s_w)
            if shift:
                p.op("vector", lambda e: e.tensor_add(tbuf, tbuf, nbuf),
                     inc=c.s_v)
                p.wait("vector", c.s_v)
                p.op("vector", lambda e: e.tensor_scalar(
                    tbuf, tbuf, PIH_HI, PIH_LO, ALU.add, ALU.add), inc=c.s_v)
            else:
                p.op("vector", lambda e: e.tensor_add(tbuf, tbuf, nbuf),
                     inc=c.s_v)
            p.wait("scalar", c.s_v)
            p.op("scalar", lambda e, d=dst, cc=cc: e.activation(
                d[:, cc * E:(cc + 1) * E], tbuf, ACT.Sin), inc=c.s_sc)

    onescol_b = c.ABF[:, 19856:19857]
    S_row = c.ABF[0:1, 19858:19986]
    NEGC = c.ABF[0:1, 16384:18432]
    p.wait("tensor", c.s_sc)
    p.wait("tensor", c.s_tr)
    p.op("vector", lambda e: e.memset(onescol_b, 1.0), inc=c.s_v)
    p.wait("tensor", c.s_v)

    def negc_mms(e):
        last_i = None
        for g, (W, half) in enumerate(((Cw, 0), (Cw, 1), (Sw, 0), (Sw, 1))):
            for cc in range(NC):
                last_i = e.matmul(
                    c.PS[4 + g][0:1, 0:512], onescol_b,
                    W[:, cc * E + half * 512:cc * E + (half + 1) * 512],
                    start=(cc == 0), stop=(cc == NC - 1))
        return last_i
    p.op("tensor", negc_mms, inc=c.s_t)
    p.wait("scalar", c.s_t)
    for g in range(4):
        p.op("scalar", lambda e, g=g: e.activation(
            NEGC[:, g * 512:(g + 1) * 512], c.PS[4 + g][0:1, 0:512],
            ACT.Identity, scale=-1.0 / 1024.0), inc=c.s_sc)
    p.wait("tensor", c.s_sc)

    for t in range(NT):
        def srow_mms(e, t=t):
            last_i = None
            for cc in range(NC):
                last_i = e.matmul(
                    c.PS[4][0:1, 0:128], onescol_b,
                    c.X_FM[:, cc * L + t * 128:cc * L + (t + 1) * 128],
                    start=(cc == 0), stop=(cc == NC - 1))
            return last_i
        p.op("tensor", srow_mms, inc=c.s_t)
        p.wait("scalar", c.s_t)
        p.op("scalar", lambda e: e.activation(
            S_row, c.PS[4][0:1, 0:128], ACT.Identity), inc=c.s_sc)
        p.wait("tensor", c.s_sc)

        def mm_block(e, t=t):
            last_i = None
            for base, W in ((0, Cw), (2, Sw)):
                for half in range(2):
                    for cc in range(NC):
                        e.matmul(
                            c.PS[base + half],
                            c.X_FM[:, cc * L + t * 128:cc * L + (t + 1) * 128],
                            W[:, cc * E + half * 512:cc * E + (half + 1) * 512],
                            start=(cc == 0), stop=False)
            for g in range(4):
                last_i = e.matmul(
                    c.PS[g], S_row, NEGC[:, g * 512:(g + 1) * 512],
                    start=False, stop=True)
            return last_i
        p.op("tensor", mm_block, inc=c.s_t)
        p.wait("scalar", c.s_t)
        for hh in range(2):
            p.op("scalar", lambda e, hh=hh: e.activation(
                re2[:, hh * 512:(hh + 1) * 512], c.PS[hh], ACT.Square),
                inc=c.s_sc)
            p.op("scalar", lambda e, hh=hh: e.activation(
                im2[:, hh * 512:(hh + 1) * 512], c.PS[2 + hh], ACT.Square),
                inc=c.s_sc)
        p.wait("vector", c.s_sc)
        p.op("vector", lambda e: e.tensor_add(re2, re2, im2), inc=c.s_v)
        p.wait("scalar", c.s_v)
        if DEV_TAP:
            p.wait("scalar", c.s_st)
        p.op("scalar", lambda e: e.activation(mag, re2, ACT.Sqrt), inc=c.s_sc)
        p.wait("vector", c.s_sc)
        emit_ln(c, t, mag[:, 0:512], mag[:, 512:1024], last)
        if DEV_TAP:
            p.wait("sync", c.s_v)
            p.wait("sync", c.s_sc)
            p.op("sync", lambda e, t=t: e.dma_start(
                out=c.dbg_d[t * 128:(t + 1) * 128, 0:1024],
                in_=c.AF32[:, 0:1024]), inc=c.s_st, n=16)
            p.op("sync", lambda e, t=t: e.dma_start(
                out=c.dbg2_d[t * 128:(t + 1) * 128, :],
                in_=c.ROW[:, 0:16]), inc=c.s_st, n=16)
            p.wait("vector", c.s_st)
            p.wait("scalar", c.s_st)
        p.wait("tensor", c.s_v)
        p.wait("tensor", c.s_sc)


def emit_l0attn(c, last):
    p = c.p
    sv = c.s_v
    WS = c.ABF[:, 0:4096]
    v65 = c.ABF[:, 4096:8256]
    stag1 = c.ABF[:, 8256:9280]
    stag2 = c.ABF[:, 9280:10304]
    stag3 = c.ABF[:, 10304:11328]
    rot_hm = c.ABF[:, 11328:15424]
    ksq = c.ABF[:, 15424:15936]
    kphi_tm = c.ABF[:, 15936:16192]
    wbuf = [c.ABF[:, 15424:15936], c.ABF[:, 15936:16448]]
    SpT = c.ABF[:, 16448:16704]
    S2 = c.ABF[:, 16768:19328]
    qphi = c.ABF[:, 19328:19584]
    oh_fm = c.ABF[:, 4096:8192]
    oh_tm = stag1
    hadb = c.ABF[:, 19584:19648]
    hadp_b = c.ABF[:, 19648:19712]
    scr_b = c.ABF[:, 19712:19776]
    hs = c.ABF[:, 19776:19840]
    asb = c.ABF[:, 19840:19904]
    c4096r = c.ABF[0:1, 19904:19968]
    frow = c.ABF[0:1, 19968:20000]

    S_acc = c.AF32[:, 2048:6144]
    sg_sb = c.AF32[:, 6144:6336]
    tgen = [c.AF32[:, 6336 + i * 32:6336 + (i + 1) * 32] for i in range(6)]
    ghb = c.AF32[:, 6528:6592]
    t_w = c.AF32[:, 6592:6656]
    NLN16 = -float(np.float32(np.log(16.0)))

    p.op("vector", lambda e: e.memset(v65, 1.0), inc=sv)
    p.op("vector", lambda e: e.memset(c4096r, 1.0 / 4096), inc=sv)
    p.op("vector", lambda e: e.memset(c.NEGH[:], -0.5), inc=sv)
    p.op("vector", lambda e: e.memset(c.ONESB_ROW, 1.0), inc=sv)
    p.op("vector", lambda e: e.memset(S_acc, 0.0), inc=sv)
    p.op("vector", lambda e: e.memset(c.ROW[:, 124:125], NLN16), inc=sv)
    p.wait("gpsimd", sv)
    p.op("gpsimd", lambda e: e.dma_start(out=sg_sb[0:64, :],
                                         in_=c.sorf_signs_t), inc=c.s_w, n=16)
    p.op("gpsimd", lambda e: e.dma_start(out=hadb[0:64, :], in_=c.had_c),
         inc=c.s_w, n=16)
    p.op("gpsimd", lambda e: e.dma_start(out=hadp_b[0:64, :], in_=c.had_p),
         inc=c.s_w, n=16)
    p.op("gpsimd", lambda e: e.dma_start(out=c.POS[:], in_=c.posc),
         inc=c.s_w, n=16)
    p.op("gpsimd", lambda e: e.dma_start(out=c.ROW[0:1, 126:127],
                                         in_=c.ten_thousand), inc=c.s_w, n=16)
    p.op("gpsimd", lambda e: e.dma_start(out=c.FREQB[0:1, :], in_=c.grid_c),
         inc=c.s_w, n=16)

    lnv = c.ROW[0:1, 125:126]
    p.wait("scalar", c.s_w)
    p.op("scalar", lambda e: e.activation(lnv, c.ROW[0:1, 126:127], ACT.Ln),
         inc=c.s_sc)
    p.wait("vector", c.s_sc)
    p.op("vector", lambda e: e.tensor_scalar(lnv, lnv, -1.0 / 32, None,
                                             ALU.mult), inc=sv)
    p.wait("scalar", sv)
    p.op("scalar", lambda e: e.activation(c.FREQB[0:1, :], c.FREQB[0:1, :],
                                          ACT.Exp, scale=lnv), inc=c.s_sc)
    p.wait("vector", c.s_sc)
    p.op("vector", lambda e: e.tensor_copy(frow, c.FREQB[0:1, :]), inc=sv)
    p.wait("tensor", sv)
    p.op("tensor", lambda e: e.matmul(c.PSA[:, 0:32], c.ONESB_ROW[0:1, 0:128],
                                      frow[0:1, 0:32],
                                      start=True, stop=True), inc=c.s_t)
    p.wait("vector", c.s_t)
    p.op("vector", lambda e: e.tensor_copy(c.FREQB[:], c.PSA[:, 0:32]),
         inc=sv)
    p.wait("vector", sv)
    for t in range(NT):
        ang, u, mf, r1, r2 = tgen[0], tgen[1], tgen[2], tgen[3], tgen[4]
        p.wait("vector", c.s_sc)
        p.op("vector", lambda e, t=t: e.tensor_scalar(
            ang, c.FREQB[:], c.POS[:, t:t + 1], None, ALU.mult), inc=sv)
        p.wait("vector", sv)
        p.op("vector", lambda e: e.tensor_scalar(u, ang, INV2PI, None,
                                                 ALU.mult), inc=sv)
        p.wait("vector", sv)
        p.op("vector", lambda e: e.tensor_scalar(
            c.AI16[:, 0:32], u, 1.0, -0.5, ALU.mult, ALU.add), inc=sv)
        p.wait("vector", sv)
        p.op("vector", lambda e: e.tensor_copy(mf, c.AI16[:, 0:32]), inc=sv)
        p.wait("vector", sv)
        p.op("vector", lambda e: e.tensor_sub(tgen[5], u, mf), inc=sv)
        p.wait("vector", sv)
        p.op("vector", lambda e: e.tensor_single_scalar(
            tgen[5], tgen[5], 0.5, ALU.is_ge), inc=sv)
        p.wait("vector", sv)
        p.op("vector", lambda e: e.tensor_add(mf, mf, tgen[5]), inc=sv)
        p.wait("vector", sv)
        p.op("vector", lambda e: e.scalar_tensor_tensor(
            r1, mf, -CW1, ang, ALU.mult, ALU.add), inc=sv)
        p.wait("vector", sv)
        p.op("vector", lambda e: e.scalar_tensor_tensor(
            r1, mf, -CW2, r1, ALU.mult, ALU.add), inc=sv)
        p.wait("vector", sv)
        p.op("vector", lambda e: e.scalar_tensor_tensor(
            r1, mf, -CW3, r1, ALU.mult, ALU.add), inc=sv)
        p.op("vector", lambda e: e.tensor_scalar(u, ang, INV2PI, 0.25,
                                                 ALU.mult, ALU.add), inc=sv)
        p.wait("vector", sv)
        p.op("vector", lambda e: e.tensor_scalar(
            c.AI16[:, 32:64], u, 1.0, -0.5, ALU.mult, ALU.add), inc=sv)
        p.wait("vector", sv)
        p.op("vector", lambda e: e.tensor_copy(mf, c.AI16[:, 32:64]), inc=sv)
        p.wait("vector", sv)
        p.op("vector", lambda e: e.tensor_sub(tgen[5], u, mf), inc=sv)
        p.wait("vector", sv)
        p.op("vector", lambda e: e.tensor_single_scalar(
            tgen[5], tgen[5], 0.5, ALU.is_ge), inc=sv)
        p.wait("vector", sv)
        p.op("vector", lambda e: e.tensor_add(mf, mf, tgen[5]), inc=sv)
        p.wait("vector", sv)
        p.op("vector", lambda e: e.scalar_tensor_tensor(
            r2, mf, -CW1, ang, ALU.mult, ALU.add), inc=sv)
        p.wait("vector", sv)
        p.op("vector", lambda e: e.scalar_tensor_tensor(
            r2, mf, -CW2, r2, ALU.mult, ALU.add), inc=sv)
        p.wait("vector", sv)
        p.op("vector", lambda e: e.scalar_tensor_tensor(
            r2, mf, -CW3, r2, ALU.mult, ALU.add), inc=sv)
        p.wait("vector", sv)
        p.op("vector", lambda e: e.tensor_scalar(
            r2, r2, PIH_HI, PIH_LO, ALU.add, ALU.add), inc=sv)
        p.wait("scalar", sv)
        p.op("scalar", lambda e, t=t: e.activation(
            c.S_all[:, t * 32:(t + 1) * 32], r1, ACT.Sin), inc=c.s_sc)
        p.op("scalar", lambda e, t=t: e.activation(
            c.C_all[:, t * 32:(t + 1) * 32], r2, ACT.Sin), inc=c.s_sc)
    p.wait("vector", c.s_sc)
    p.op("vector", lambda e: e.memset(c.C_all[0:1, 0:32], 1.0), inc=sv)
    p.op("vector", lambda e: e.memset(c.S_all[0:1, 0:32], 0.0), inc=sv)

    p.wait("vector", c.s_w)
    for h in range(H):
        for r in range(4):
            col = h * 12 + r * 3
            p.wait("vector", c.s_t)
            p.op("vector", lambda e, col=col: e.tensor_scalar(
                hs[0:64, :], hadb[0:64, :], sg_sb[0:64, col + 2:col + 3],
                None, ALU.mult), inc=sv)
            p.wait("tensor", sv)
            p.op("tensor", lambda e: e.matmul(
                c.PSA[0:64, 0:64], hs[0:64, :], hadb[0:64, :],
                start=True, stop=True), inc=c.s_t)
            p.wait("vector", c.s_t)
            p.op("vector", lambda e: e.tensor_copy(asb[0:64, :],
                                                   c.PSA[0:64, 0:64]), inc=sv)
            p.wait("vector", sv)
            p.op("vector", lambda e, col=col: e.tensor_scalar(
                hs[0:64, :], hadp_b[0:64, :], sg_sb[0:64, col + 1:col + 2],
                None, ALU.mult), inc=sv)
            p.wait("tensor", sv)
            p.op("tensor", lambda e: e.matmul(
                c.PSA[0:64, 64:128], hs[0:64, :], asb[0:64, :],
                start=True, stop=True), inc=c.s_t)
            p.wait("gpsimd", c.s_t)
            p.op("gpsimd", lambda e, h=h, r=r: e.dma_start(
                out=scr_b[0:1, 0:64],
                in_=c.sorf_scales_r[h * 4 + r:h * 4 + r + 1, :]),
                inc=c.s_w, n=16)
            p.wait("tensor", c.s_w)
            p.op("tensor", lambda e: e.matmul(
                c.PSA[0:64, 128:192], c4096r, scr_b[0:1, 0:64],
                start=True, stop=True), inc=c.s_t)
            p.wait("vector", c.s_t)
            p.op("vector", lambda e: e.tensor_copy(ghb[0:64, :],
                                                   c.PSA[0:64, 128:192]),
                 inc=sv)
            p.op("vector", lambda e, col=col: e.tensor_scalar(
                t_w[0:64, :], c.PSA[0:64, 64:128],
                sg_sb[0:64, col + 0:col + 1], None, ALU.mult), inc=sv)
            p.wait("vector", sv)
            p.op("vector", lambda e, h=h, r=r: e.tensor_mul(
                WS[0:64, h * 256 + r * 64:h * 256 + (r + 1) * 64],
                t_w[0:64, :], ghb[0:64, :]), inc=sv)
            p.wait("tensor", sv)

    krv = c.kr_d.rearrange("p (cc t) -> p cc t", cc=NC)
    qrv = c.qr_d.rearrange("p (cc t) -> p cc t", cc=NC)
    rotv = rot_hm.rearrange("p (h t) -> p h t", h=16)
    v65v = v65.rearrange("p (s w) -> p s w", s=4)
    S2v = S2.rearrange("p (h j w) -> p h j w", h=16, j=2)
    CSbc = lambda T, tt: (T[:, tt * 32:(tt + 1) * 32]
                          .unsqueeze(1).broadcast_to([128, 16, 32]))
    mmld = []

    def rope_and_bounce(tt, rd):
        sh = lambda buf: buf.rearrange("p (h two q) -> p h two q", h=16, two=2)
        ev, od = sh(stag1)[:, :, 0, :], sh(stag1)[:, :, 1, :]
        rev, rod = sh(stag2)[:, :, 0, :], sh(stag2)[:, :, 1, :]
        tmp1 = stag3[:, 0:512].rearrange("p (h q) -> p h q", h=16)
        tmp2 = stag3[:, 512:1024].rearrange("p (h q) -> p h q", h=16)
        Cb, Sb = CSbc(c.C_all, tt), CSbc(c.S_all, tt)
        p.op("vector", lambda e: e.tensor_tensor(rev, ev, Cb, ALU.mult),
             inc=sv)
        p.op("vector", lambda e: e.tensor_tensor(tmp1, od, Sb, ALU.mult),
             inc=sv)
        p.op("vector", lambda e: e.tensor_tensor(tmp2, ev, Sb, ALU.mult),
             inc=sv)
        p.op("vector", lambda e: e.tensor_tensor(rod, od, Cb, ALU.mult),
             inc=sv)
        p.wait("vector", sv)
        p.op("vector", lambda e: e.tensor_sub(rev, rev, tmp1), inc=sv)
        p.op("vector", lambda e: e.tensor_add(rod, rod, tmp2), inc=sv)
        p.wait("sync", sv)
        for cc in range(NC):
            p.op("sync", lambda e, cc=cc: e.dma_start(
                out=stag3[:, cc * 128:(cc + 1) * 128],
                in_=stag2[:, cc * 128:(cc + 1) * 128], transpose=True),
                inc=c.s_tr, n=16)
        p.wait("sync", c.s_tr)
        p.op("sync", lambda e, tt=tt, rd=rd: e.dma_start(
            out=rd.rearrange("p (cc t) -> p cc t", cc=NC)[
                :, :, tt * 128:(tt + 1) * 128],
            in_=stag3.rearrange("p (cc t) -> p cc t", cc=NC)),
            inc=c.s_ws2, n=16)

    def produce_tm(T0, woff, kind):
        p.wait("gpsimd", c.s_t)
        for half in range(2):
            for cc in range(NC):
                par = len(mmld) % 2
                if len(mmld) >= 2:
                    p.wait("gpsimd", c.s_t, mmld[-2])
                p.op("gpsimd", lambda e, woff=woff, half=half, cc=cc,
                     par=par: e.dma_start(
                         out=wbuf[par],
                         in_=c.w_qkv0T[cc * 128:(cc + 1) * 128,
                                       woff + half * 512:
                                       woff + (half + 1) * 512]),
                     inc=c.s_wp[par], n=16)
                p.wait("tensor", c.s_wp[par])
                def mmf(e, half=half, cc=cc, T0=T0, par=par):
                    li = None
                    for s_ in range(4):
                        tt = T0 // 128 + s_
                        li = e.matmul(
                            c.PS[s_ + 4 * half],
                            c.X_FM[:, cc * L + tt * 128:
                                   cc * L + (tt + 1) * 128],
                            wbuf[par],
                            start=(cc == 0), stop=(cc == NC - 1))
                    return li
                p.op("tensor", mmf, inc=c.s_t)
                mmld.append(c.s_t.n)
        for s_ in range(4):
            tt = T0 // 128 + s_
            if kind == "v":
                p.wait("scalar", c.s_t)
                for half in range(2):
                    p.op("scalar", lambda e, s_=s_, half=half: e.activation(
                        v65v[:, s_, half * 520:half * 520 + 520]
                        .rearrange("p (h w) -> p h w", h=8)[:, :, 0:64],
                        c.PS[s_ + 4 * half], ACT.Identity), inc=c.s_sc)
            else:
                p.wait("scalar", c.s_t)
                p.wait("scalar", c.s_ws2)
                p.wait("scalar", sv)
                p.wait("scalar", c.s_tr)
                for half in range(2):
                    p.op("scalar", lambda e, s_=s_, half=half: e.activation(
                        stag1[:, half * 512:(half + 1) * 512],
                        c.PS[s_ + 4 * half], ACT.Identity), inc=c.s_sc)
                p.wait("vector", c.s_sc)
                p.wait("vector", c.s_tr)
                rope_and_bounce(tt, c.kr_d if kind == "k" else c.qr_d)
        p.wait("tensor", c.s_sc)
        if kind != "v":
            p.wait("tensor", sv)

    for t in range(NT // 4):
        T0 = t * 512
        produce_tm(T0, E, "k")
        produce_tm(T0, 2 * E, "v")
        p.wait("gpsimd", c.s_ws2)
        for pair in range(2):
            Tp = T0 + pair * 256
            p.wait("gpsimd", c.s_t)
            for hp in range(2):
                p.op("gpsimd", lambda e, Tp=Tp, hp=hp: e.dma_start(
                    out=rotv[0:64, hp::2, :],
                    in_=krv[hp * 64:hp * 64 + 64, :, Tp:Tp + 256]),
                    inc=c.s_w, n=16)
            p.wait("vector", c.s_w)
            for h in range(H):
                p.wait("vector", c.s_t)
                p.op("vector", lambda e, h=h: e.tensor_tensor(
                    ksq[0:64, 0:256], rotv[0:64, h, :],
                    rotv[0:64, h, :], ALU.mult), inc=sv)
                p.wait("tensor", sv)
                for sp in range(2):
                    s_ = pair * 2 + sp
                    p.wait("tensor", c.s_sc)
                    def phim(e, h=h, sp=sp):
                        e.matmul(c.PSB2[:, 0:256],
                                 rotv[0:64, h, sp * 128:(sp + 1) * 128],
                                 WS[0:64, h * 256:(h + 1) * 256],
                                 start=True, stop=False)
                        return e.matmul(
                            c.PSB2[:, 0:256],
                            ksq[0:64, sp * 128:(sp + 1) * 128],
                            c.NEGH[0:64, 0:256],
                            start=False, stop=True)
                    p.op("tensor", phim, inc=c.s_t)
                    p.wait("scalar", c.s_t)
                    p.op("scalar", lambda e: e.activation(
                        kphi_tm, c.PSB2[:, 0:256], ACT.Exp,
                        bias=c.ROW[:, 124:125]), inc=c.s_sc)
                    p.wait("tensor", c.s_sc)
                    def smm2(e, h=h, s_=s_, sp=sp):
                        return e.matmul(
                            c.PSB2[0:65, 512:768],
                            v65v[:, s_, h * 65:h * 65 + 65],
                            kphi_tm,
                            start=(sp == 0), stop=(sp == 1))
                    p.op("tensor", smm2, inc=c.s_t)
                p.wait("vector", c.s_t)
                p.op("vector", lambda e, h=h: e.tensor_add(
                    S_acc[0:65, h * 256:(h + 1) * 256],
                    S_acc[0:65, h * 256:(h + 1) * 256],
                    c.PSB2[0:65, 512:768]), inc=sv)
                p.wait("tensor", sv)

    p.wait("vector", c.s_t)
    p.op("vector", lambda e: e.memset(SpT[:], 0.0), inc=sv)
    p.wait("vector", sv)
    for h in range(H):
        p.wait("vector", sv)
        p.op("vector", lambda e, h=h: e.tensor_copy(
            SpT[0:65, :], S_acc[0:65, h * 256:(h + 1) * 256]), inc=sv)
        p.wait("sync", sv)
        p.wait("sync", c.s_t)
        for j in range(2):
            p.op("sync", lambda e, h=h, j=j: e.dma_start(
                out=S2v[:, h, j, 0:80],
                in_=SpT[0:80, j * 128:(j + 1) * 128], transpose=True),
                inc=c.s_tr, n=16)
        p.wait("vector", c.s_tr)

    nln16c = NLN16
    for t in range(NT // 4):
        T0 = t * 512
        produce_tm(T0, 0, "q")
        p.wait("gpsimd", c.s_ws2)
        for pair in range(2):
            Tp = T0 + pair * 256
            p.wait("gpsimd", c.s_t)
            for hp in range(2):
                p.op("gpsimd", lambda e, Tp=Tp, hp=hp: e.dma_start(
                    out=rotv[0:64, hp::2, :],
                    in_=qrv[hp * 64:hp * 64 + 64, :, Tp:Tp + 256]),
                    inc=c.s_w, n=16)
            p.wait("vector", c.s_w)
            for sp in range(2):
                s_ = pair * 2 + sp
                tt = T0 // 128 + s_
                for h in range(H):
                    p.wait("vector", c.s_t)
                    p.op("vector", lambda e, h=h, sp=sp: e.tensor_tensor(
                        ksq[0:64, 0:128], rotv[0:64, h, sp * 128:(sp + 1) * 128],
                        rotv[0:64, h, sp * 128:(sp + 1) * 128], ALU.mult),
                        inc=sv)
                    p.wait("tensor", sv)
                    p.wait("tensor", c.s_sc)
                    def qphim(e, h=h, sp=sp):
                        for j in range(2):
                            e.matmul(c.PSB2[:, 0:128 if j == 0 else 256][
                                :, j * 128:(j + 1) * 128] if False else
                                c.PSB2[:, j * 128:(j + 1) * 128],
                                WS[0:64, h * 256 + j * 128:
                                   h * 256 + (j + 1) * 128],
                                rotv[0:64, h, sp * 128:(sp + 1) * 128],
                                start=True, stop=False)
                            li = e.matmul(
                                c.PSB2[:, j * 128:(j + 1) * 128],
                                c.NEGH[0:64, 0:128],
                                ksq[0:64, 0:128],
                                start=False, stop=True)
                        return li
                    p.op("tensor", qphim, inc=c.s_t)
                    p.wait("scalar", c.s_t)
                    p.op("scalar", lambda e: e.activation(
                        qphi, c.PSB2[:, 0:256], ACT.Exp,
                        bias=c.ROW[:, 124:125]), inc=c.s_sc)
                    p.wait("tensor", c.s_sc)
                    def nmm(e, h=h):
                        for j in range(2):
                            li = e.matmul(
                                c.PSA[:, h * 128:h * 128 + 65],
                                qphi[:, j * 128:(j + 1) * 128],
                                S2v[:, h, j, 0:65],
                                start=(j == 0), stop=(j == 1))
                        return li
                    p.op("tensor", nmm, inc=c.s_t)
                p.wait("vector", c.s_t)
                nview = c.PSA.rearrange("p (h w) -> p h w", h=16)
                p.op("vector", lambda e: e.tensor_scalar(
                    c.ROW[:, 108:124], nview[:, :, 64], 1e-6, None, ALU.max),
                    inc=sv)
                p.wait("vector", sv)
                p.op("vector", lambda e: e.reciprocal(
                    c.ROW[:, 108:124], c.ROW[:, 108:124]), inc=sv)
                p.wait("vector", sv)
                p.wait("vector", c.s_tr)
                p.wait("vector", c.s_ws2)
                p.wait("vector", c.s_sc)
                p.op("vector", lambda e: e.tensor_tensor(
                    oh_tm.rearrange("p (h w) -> p h w", h=16),
                    nview[:, :, 0:64],
                    c.ROW[:, 108:124].unsqueeze(2).broadcast_to([128, 16, 64]),
                    ALU.mult), inc=sv)
                p.wait("sync", sv)
                p.wait("sync", c.s_t)
                for cc in range(NC):
                    p.op("sync", lambda e, s_=s_, cc=cc: e.dma_start(
                        out=oh_fm[:, cc * 512 + s_ * 128:
                                  cc * 512 + (s_ + 1) * 128],
                        in_=oh_tm[:, cc * 128:(cc + 1) * 128],
                        transpose=True), inc=c.s_tr, n=16)
                p.wait("tensor", sv)
        p.wait("tensor", c.s_tr)
        p.wait("gpsimd", c.s_t)
        prj = []
        for k in range(NC):
            for hf in range(2):
                par = len(prj) % 2
                if len(prj) >= 2:
                    p.wait("gpsimd", c.s_t, prj[-2])
                p.op("gpsimd", lambda e, k=k, par=par, hf=hf: e.dma_start(
                    out=wbuf[par],
                    in_=c.w_out0T[k * 128:(k + 1) * 128,
                                  hf * 512:(hf + 1) * 512]),
                    inc=c.s_wp[par], n=16)
                p.wait("tensor", c.s_wp[par])
                def pjm(e, k=k, par=par, hf=hf):
                    li = None
                    for s_ in range(4):
                        li = e.matmul(
                            c.PS[2 * s_ + hf],
                            oh_fm[:, k * 512 + s_ * 128:
                                  k * 512 + (s_ + 1) * 128],
                            wbuf[par],
                            start=(k == 0), stop=(k == NC - 1))
                    return li
                p.op("tensor", pjm, inc=c.s_t)
                prj.append(c.s_t.n)
        for s_ in range(4):
            p.wait("vector", c.s_t)
            emit_ln(c, 4 * t + s_, c.PS[2 * s_], c.PS[2 * s_ + 1], last)
        p.wait("tensor", sv)
        p.wait("tensor", c.s_sc)
        p.wait("sync", sv)


def emit_l1attn(c, last):
    p = c.p
    q_sb = c.ABF[:, 0:4096]
    kv_sb = c.ABF[:, 4096:8192]
    lat_sb = c.ABF[:, 8192:12288]
    vtm_sb = c.ABF[:, 12288:16384]
    wbuf = [c.ABF[:, 16384 + i * 512:16384 + (i + 1) * 512] for i in range(4)]
    W2 = c.ABF[:, 18432:18560]
    q2h = c.ABF[:, 0:2048]
    span = c.ABF[:, 2048:4224]
    vlat2 = c.ABF[:, 4224:5248]
    vedge = c.ABF[:, 5248:6272]
    Pst = c.ABF[:, 6272:7552]
    PT1 = c.ABF[:, 7552:7680]
    PT2 = c.ABF[:, 7680:7808]
    oh_tm = c.ABF[:, 7808:8832]
    kedgeL = c.ABF[:, 8832:8848]
    kedge0 = c.ABF[:, 8848:8864]
    oh_fm = c.ABF[:, 12288:16384]
    bq_c = c.AF32[:, 3072:3096]
    blat2 = c.AF32[:, 3096:3097]
    negmax = c.ROW[:, 8:16]
    sums8 = c.ROW[:, 64:80]
    recs = c.ROW[:, 80:96]

    p.op("vector", lambda e: e.memset(W2, 0.0), inc=c.s_v)
    p.op("vector", lambda e: e.memset(Pst, 0.0), inc=c.s_v)
    p.op("vector", lambda e: e.memset(vedge, 0.0), inc=c.s_v)
    p.op("vector", lambda e: e.memset(c.ONESB_ROW, 1.0), inc=c.s_v)
    p.wait("gpsimd", c.s_v)
    prep_ops = [
        lambda e: e.dma_start(out=c.ABF[0:64, 18432:18496], in_=c.w_latT),
        lambda e: e.dma_start(out=c.ABF[64:128, 18496:18560], in_=c.w_latT),
        lambda e: e.dma_start(out=bq_c, in_=c.b_qkv1c),
        lambda e: e.dma_start(out=blat2, in_=c.b_lat2),
        lambda e: e.dma_start(out=c.BPROJR, in_=c.b_proj1),
        lambda e: e.dma_start(out=c.MASKB, in_=c.masks_c),
    ]
    for fop in prep_ops[:L1A_PREP]:
        p.op("gpsimd", fop, inc=c.s_w, n=16)

    p.wait("tensor", c.s_w)
    p.wait("tensor", c.s_v)
    ld_done = []
    for t in range((NT // 4) if L1A_P1PARTS > 0 else 0):
        T0 = t * 512
        for part, (dst, bias_off) in enumerate(
                ((q_sb, 0), (kv_sb, 8), (kv_sb, 16))[:L1A_P1PARTS]):
            for oc in range(NC):
                for k in range(NC):
                    par = len(ld_done) % 2
                    wb = wbuf[par]
                    if len(ld_done) >= 2:
                        p.wait("gpsimd", c.s_t, ld_done[-2])
                    p.op("gpsimd", lambda e, k=k, oc=oc, part=part, wb=wb:
                         e.dma_start(out=wb[:, 0:128], in_=c.w_qkv1T[
                             k * 128:(k + 1) * 128,
                             part * E + oc * 128:part * E + (oc + 1) * 128]),
                         inc=c.s_wp[par], n=16)
                    p.wait("tensor", c.s_wp[par])
                    p.op("tensor", lambda e, k=k, oc=oc, T0=T0, wb=wb:
                         e.matmul(c.PS[oc % 4],
                                  wb[:, 0:128],
                                  c.X_FM[:, k * L + T0:k * L + T0 + 512],
                                  start=(k == 0), stop=(k == NC - 1)),
                         inc=c.s_t)
                    ld_done.append(c.s_t.n)
                if oc % 4 == 3:
                    p.wait("scalar", c.s_t)
                    for jj in range(4):
                        occ = oc - 3 + jj
                        p.op("scalar", lambda e, occ=occ, jj=jj, dst=dst,
                             bias_off=bias_off: e.activation(
                                 dst[:, occ * 512:(occ + 1) * 512],
                                 c.PS[jj], ACT.Identity,
                                 bias=bq_c[:, bias_off + occ:
                                           bias_off + occ + 1]), inc=c.s_sc)
                    p.wait("tensor", c.s_sc)
            if part == 0:
                p.wait("sync", c.s_sc)
                for cc in range(NC):
                    p.op("sync", lambda e, T0=T0, cc=cc: e.dma_start(
                        out=c.qf_d[:, cc * L + T0:cc * L + T0 + 512],
                        in_=q_sb[:, cc * 512:(cc + 1) * 512]),
                        inc=c.s_ws2, n=16)
            else:
                p.wait("scalar", c.s_ws2)
                p.wait("scalar", c.s_v)
                for cc in range(NC):
                    p.wait("tensor", c.s_sc)
                    p.op("tensor", lambda e, cc=cc: e.matmul(
                        c.PS[4 + cc % 2], W2,
                        kv_sb[:, cc * 512:(cc + 1) * 512],
                        start=True, stop=True), inc=c.s_t)
                    p.wait("scalar", c.s_t)
                    p.op("scalar", lambda e, cc=cc: e.activation(
                        lat_sb[:, cc * 512:(cc + 1) * 512],
                        c.PS[4 + cc % 2], ACT.Identity,
                        bias=blat2), inc=c.s_sc)
                p.wait("sync", c.s_sc)
                if part == 1:
                    for cc in range(NC):
                        p.op("sync", lambda e, T0=T0, cc=cc: e.dma_start(
                            out=c.kl_d[:, cc * L + T0:cc * L + T0 + 512],
                            in_=lat_sb[:, cc * 512:(cc + 1) * 512]),
                            inc=c.s_ws2, n=16)

                else:
                    for jj in range(4):
                        for cc in range(NC):
                            p.op("sync", lambda e, jj=jj, cc=cc: e.dma_start(
                                out=vtm_sb[:, jj * 1024 + cc * 128:
                                           jj * 1024 + (cc + 1) * 128],
                                in_=lat_sb[:, cc * 512 + jj * 128:
                                           cc * 512 + (jj + 1) * 128],
                                transpose=True), inc=c.s_tr, n=16)
                    p.wait("sync", c.s_tr)
                    for jj in range(4):
                        p.op("sync", lambda e, T0=T0, jj=jj: e.dma_start(
                            out=c.vl_d[T0 + jj * 128:T0 + (jj + 1) * 128, :],
                            in_=vtm_sb[:, jj * 1024:(jj + 1) * 1024]),
                            inc=c.s_ws2, n=16)
            p.wait("scalar", c.s_t)
        p.wait("scalar", c.s_ws2)
        p.wait("sync", c.s_ws2)

    p.wait("gpsimd", c.s_ws2)
    p.wait("vector", c.s_ws2)
    p.op("vector", lambda e: e.memset(Pst, 0.0), inc=c.s_v)
    p.op("vector", lambda e: e.memset(vedge, 0.0), inc=c.s_v)
    p.wait("gpsimd", c.s_v)
    p.wait("sync", c.s_v)
    qfv = c.qf_d.rearrange("p (c t) -> p c t", c=NC)
    klv0 = c.kl_d.rearrange("p (c t) -> p c t", c=NC)
    kevw = [kedgeL.rearrange("p (h x) -> p h x", h=16),
            kedge0.rearrange("p (h x) -> p h x", h=16)]
    for which, tok in ((0, L - 1), (1, 0)):
        for hp in range(2):
            p.op("gpsimd", lambda e, which=which, tok=tok, hp=hp: e.dma_start(
                out=kevw[which][0:64, hp::2, 0:1],
                in_=klv0[hp * 64:hp * 64 + 64, :, tok:tok + 1],
                allow_slow_non_contiguous=True), inc=c.s_w, n=16)
    klv = c.kl_d.rearrange("p (c t) -> p c t", c=NC)
    spanv = span.rearrange("p (h t) -> p h t", h=16)
    q2v = q2h.rearrange("p (h t) -> p h t", h=16)
    for j in range(NT if L1A_LIMIT is None else L1A_LIMIT):
        t0 = j * 128
        p.wait("gpsimd", c.s_t)
        p.wait("gpsimd", c.s_tr)
        for hp in range(2):
            p.op("gpsimd", lambda e, t0=t0, hp=hp: e.dma_start(
                out=q2v[0:64, hp::2, :], in_=qfv[hp * 64:hp * 64 + 64,
                                                :, t0:t0 + 128]),
                inc=c.s_w, n=16)
        lo = max(t0 - 3, 0)
        hi = min(t0 + 131, L)
        off = lo - (t0 - 3)
        wid = hi - lo
        for hp in range(2):
            p.op("gpsimd", lambda e, lo=lo, off=off, wid=wid, hp=hp:
                 e.dma_start(out=spanv[0:64, hp::2, off:off + wid],
                             in_=klv[hp * 64:hp * 64 + 64, :, lo:lo + wid]),
                 inc=c.s_w, n=16)
        need_left_fill = bool(off)
        need_right_fill = hi < t0 + 131
        if t0 >= 3:
            p.op("gpsimd", lambda e, t0=t0: e.dma_start(
                out=vlat2, in_=c.vl_d[t0 - 3:t0 + 125, :]), inc=c.s_w, n=16)
        else:
            for i in range(3):
                p.op("gpsimd", lambda e, i=i: e.dma_start(
                    out=vlat2.rearrange("p (a b) -> p a b", a=1)[
                        i:i + 1, 0, 0:1024] if False else
                    vlat2[i:i + 1, 0:1024], in_=c.vl_d[L - 1:L, :]),
                    inc=c.s_w, n=16)
            p.op("gpsimd", lambda e: e.dma_start(
                out=vlat2[3:128, 0:1024], in_=c.vl_d[0:125, :]),
                inc=c.s_w, n=16)
        if t0 + 131 <= L:
            p.op("gpsimd", lambda e, t0=t0: e.dma_start(
                out=vedge[0:6, 0:1024], in_=c.vl_d[t0 + 125:t0 + 131, :]),
                inc=c.s_w, n=16)
        else:
            p.op("gpsimd", lambda e, t0=t0: e.dma_start(
                out=vedge[0:3, 0:1024], in_=c.vl_d[t0 + 125:t0 + 128, :]),
                inc=c.s_w, n=16)
            for i in range(3):
                p.op("gpsimd", lambda e, i=i: e.dma_start(
                    out=vedge[3 + i:4 + i, 0:1024], in_=c.vl_d[0:1, :]),
                    inc=c.s_w, n=16)
        if need_left_fill or need_right_fill:
            p.wait("vector", c.s_w)
            if need_left_fill:
                p.op("vector", lambda e: e.tensor_copy(
                    spanv[0:64, :, 0:3],
                    kedgeL.rearrange("p (h x) -> p h x", h=16)[
                        0:64, :, 0:1].broadcast_to([64, 16, 3])), inc=c.s_v)
            if need_right_fill:
                p.op("vector", lambda e: e.tensor_copy(
                    spanv[0:64, :, 131:134],
                    kedge0.rearrange("p (h x) -> p h x", h=16)[
                        0:64, :, 0:1].broadcast_to([64, 16, 3])), inc=c.s_v)
        p.wait("tensor", c.s_w)
        p.wait("tensor", c.s_v)
        if L1A_SUB < 2:
            continue
        for g in range(2):
            p.wait("tensor", c.s_sc)
            p.wait("tensor", c.s_v)
            def smm(e, g=g):
                li = None
                for i in range(8):
                    h = g * 8 + i
                    li = e.matmul(
                        c.PSA[:, i * 256:i * 256 + 134],
                        q2v[0:64, h, :],
                        spanv[0:64, h, 0:134],
                        start=True, stop=True)
                return li
            p.op("tensor", smm, inc=c.s_t)
            if L1A_SUB < 3:
                continue
            p.wait("vector", c.s_t)
            p.op("vector", lambda e: e.tensor_reduce(
                negmax, c.PSA.rearrange("p (i w) -> p i w", i=8)[:, :, 0:134],
                AX.X, ALU.max, negate=True), inc=c.s_v)
            p.wait("vector", c.s_v)
            p.op("vector", lambda e: e.tensor_scalar(
                negmax, negmax, 0.125, None, ALU.mult), inc=c.s_v)
            p.wait("scalar", c.s_v)
            p.wait("scalar", c.s_tr)
            for i in range(8):
                p.op("scalar", lambda e, i=i: e.activation(
                    Pst[:, i * 144:i * 144 + 134],
                    c.PSA[:, i * 256:i * 256 + 134], ACT.Exp,
                    bias=negmax[:, i:i + 1], scale=0.125), inc=c.s_sc)
            p.wait("vector", c.s_sc)
            pstv = Pst[:, 0:1152].rearrange("p (i w) -> p i w", i=8)
            p.op("vector", lambda e: e.tensor_tensor(
                pstv[:, 0:8, 0:134], pstv[:, 0:8, 0:134],
                c.MASKB[:, 0:134].unsqueeze(1).broadcast_to([128, 8, 134]),
                ALU.mult), inc=c.s_v)
            p.wait("vector", c.s_v)
            p.op("vector", lambda e, g=g: e.tensor_reduce(
                sums8[:, g * 8:(g + 1) * 8], pstv[:, 0:8, 0:134],
                AX.X, ALU.add), inc=c.s_v)
            p.wait("sync", c.s_v)
            if L1A_SUB < 4:
                continue
            for i in range(8):
                h = g * 8 + i
                if not (g == 0 and i == 0):
                    p.wait("sync", c.s_t)
                p.op("sync", lambda e, i=i: e.dma_start(
                    out=PT1, in_=Pst[:, i * 144:i * 144 + 128],
                    transpose=True), inc=c.s_tr, n=16)
                p.op("sync", lambda e, i=i: e.dma_start(
                    out=PT2, in_=Pst[:, i * 144 + 128:i * 144 + 256],
                    transpose=True), inc=c.s_tr, n=16)
                p.wait("tensor", c.s_tr)
                def pv(e, h=h):
                    e.matmul(c.PSB2[:, h * 64:(h + 1) * 64], PT1,
                             vlat2[:, h * 64:(h + 1) * 64],
                             start=True, stop=False)
                    return e.matmul(c.PSB2[:, h * 64:(h + 1) * 64],
                                    PT2[0:8, :],
                                    vedge[0:8, h * 64:(h + 1) * 64],
                                    start=False, stop=True)
                p.op("tensor", pv, inc=c.s_t)
        if L1A_SUB < 5:
            continue
        p.wait("vector", c.s_t)
        p.op("vector", lambda e: e.reciprocal(recs, sums8), inc=c.s_v)
        p.wait("vector", c.s_v)
        p.op("vector", lambda e: e.tensor_tensor(
            oh_tm.rearrange("p (h d) -> p h d", h=16),
            c.PSB2[:, 0:1024].rearrange("p (h d) -> p h d", h=16),
            recs.unsqueeze(2).broadcast_to([128, 16, 64]), ALU.mult),
            inc=c.s_v)
        p.wait("sync", c.s_v)
        p.wait("sync", c.s_t)
        jj = j % 4
        for cc in range(NC):
            p.op("sync", lambda e, jj=jj, cc=cc: e.dma_start(
                out=oh_fm[:, cc * 512 + jj * 128:cc * 512 + (jj + 1) * 128],
                in_=oh_tm[:, cc * 128:(cc + 1) * 128], transpose=True),
                inc=c.s_tr, n=16)
        p.wait("tensor", c.s_v)
        if jj == 3:
            p.wait("tensor", c.s_tr)
            prj_done = []
            for k in range(NC):
                par = k % 2
                if len(prj_done) >= 2:
                    p.wait("gpsimd", c.s_t, prj_done[-2])
                p.op("gpsimd", lambda e, k=k, par=par: e.dma_start(
                    out=wbuf[2 * par], in_=c.w_proj1T[
                        k * 128:(k + 1) * 128, 0:512]),
                    inc=c.s_wp[par], n=16)
                p.op("gpsimd", lambda e, k=k, par=par: e.dma_start(
                    out=wbuf[2 * par + 1], in_=c.w_proj1T[
                        k * 128:(k + 1) * 128, 512:1024]),
                    inc=c.s_wp[par], n=16)
                p.wait("tensor", c.s_wp[par])
                def pmm(e, k=k, par=par):
                    li = None
                    for ts_ in range(4):
                        for hf in range(2):
                            li = e.matmul(
                                c.PS[2 * ts_ + hf],
                                oh_fm[:, k * 512 + ts_ * 128:
                                      k * 512 + (ts_ + 1) * 128],
                                wbuf[2 * par + hf],
                                start=(k == 0), stop=False)
                    return li
                p.op("tensor", pmm, inc=c.s_t)
                prj_done.append(c.s_t.n)
            def bias_mms(e):
                li = None
                for ts_ in range(4):
                    for hf in range(2):
                        li = e.matmul(c.PS[2 * ts_ + hf], c.ONESB_ROW,
                                      c.BPROJR[0:1, hf * 512:(hf + 1) * 512],
                                      start=False, stop=True)
                return li
            p.op("tensor", bias_mms, inc=c.s_t)
            for ts_ in range(4):
                p.wait("vector", c.s_t)
                emit_ln(c, j - 3 + ts_, c.PS[2 * ts_], c.PS[2 * ts_ + 1],
                        last)
            p.wait("tensor", c.s_v)
            p.wait("tensor", c.s_sc)
            p.wait("sync", c.s_v)


def emit_l1ffn(c, last):
    p = c.p
    hB = c.ABF[:, 0:16384]
    wbuf = [c.ABF[:, 16384 + i * 512:16384 + (i + 1) * 512] for i in range(4)]
    rgb = c.ABF[:, 18432:20480]
    qbuf = c.ABF[:, 18432:19456]
    bf_a = c.ABF[:, 19456:20480]
    fchunk = c.AF32[:, 2048:3072]
    t1buf = c.AF32[:, 4096:4608]

    wsum = c.ROW[:, 8:9]
    red1 = c.ROW[:, 9:10]
    mb_g, mb_u, mb_d = (c.ROW[:, i:i + 1] for i in (10, 11, 12))
    sw_col = c.ROW[:, 13:14]
    th_col = c.ROW[:, 14:15]
    dq_col = c.ROW[:, 15:16]
    sa_col = c.ROW[:, 16:17]
    tot_sb = c.ROW[:, 17:18]
    red2 = c.ROW[:, 22:23]
    tx_all = c.ROW[:, 32:64]

    for (wsrc, qdst, mb, kchunks) in ((c.w_gateT, c.qgate_d, mb_g, 8),
                                      (c.w_upT, c.qup_d, mb_u, 8),
                                      (c.w_downT, c.qdown_d, mb_d, 32)):
        ncol = wsrc.shape[1]
        quarters = ncol // 1024
        p.op("vector", lambda e: e.memset(wsum, 0.0), inc=c.s_v)
        for k in range(kchunks):
            for qq in range(quarters):
                p.wait("gpsimd", c.s_v)
                p.op("gpsimd", lambda e, w=wsrc, k=k, qq=qq: e.dma_start(
                    out=fchunk, in_=w[k * 128:(k + 1) * 128,
                                      qq * 1024:(qq + 1) * 1024]),
                    inc=c.s_w, n=16)
                p.wait("vector", c.s_w)
                p.op("vector", lambda e: e.reduce_sum(
                    red1, fchunk, AX.X, apply_absolute_value=True), inc=c.s_v)
                p.wait("vector", c.s_v)
                p.op("vector", lambda e: e.tensor_add(wsum, wsum, red1),
                     inc=c.s_v)
        p.wait("tensor", c.s_v)
        p.op("tensor", lambda e: e.matmul(c.PS0_11, c.ONESF_COL,
                                          wsum, start=True, stop=True),
             inc=c.s_t)
        p.wait("vector", c.s_t)
        p.op("vector", lambda e: e.tensor_copy(tot_sb[0:1, :],
                                               c.PS0_11), inc=c.s_v)
        p.wait("tensor", c.s_v)
        p.op("tensor", lambda e: e.matmul(c.PS0_C1, c.ONESF_ROW,
                                          tot_sb[0:1, :], start=True,
                                          stop=True), inc=c.s_t)
        p.wait("vector", c.s_t)
        p.op("vector", lambda e, n=float(wsrc.shape[0] * ncol): e.tensor_scalar(
            red2, c.PS0_C1, 1.0 / n, None, ALU.mult), inc=c.s_v)
        p.wait("vector", c.s_v)
        p.op("vector", lambda e, mb=mb: e.tensor_scalar(
            mb, red2, 1e-5, None, ALU.max), inc=c.s_v)
        p.wait("vector", c.s_v)
        p.op("vector", lambda e, mb=mb: e.reciprocal(sw_col, mb), inc=c.s_v)
        for k in range(kchunks):
            for qq in range(quarters):
                p.wait("gpsimd", c.s_v)
                p.op("gpsimd", lambda e, w=wsrc, k=k, qq=qq: e.dma_start(
                    out=fchunk, in_=w[k * 128:(k + 1) * 128,
                                      qq * 1024:(qq + 1) * 1024]),
                    inc=c.s_w, n=16)
                p.wait("vector", c.s_w)
                p.op("vector", lambda e: e.tensor_scalar(
                    c.AI16[:], fchunk, sw_col, -0.5, ALU.mult, ALU.add),
                    inc=c.s_v)
                p.wait("vector", c.s_v)
                p.op("vector", lambda e: e.tensor_copy(bf_a, c.AI16[:]),
                     inc=c.s_v)
                p.wait("vector", c.s_v)
                p.op("vector", lambda e: e.scalar_tensor_tensor(
                    fchunk, fchunk, sw_col, bf_a, ALU.mult, ALU.subtract),
                    inc=c.s_v)
                p.wait("vector", c.s_v)
                p.op("vector", lambda e: e.tensor_single_scalar(
                    fchunk, fchunk, 0.5, ALU.is_ge), inc=c.s_v)
                p.wait("vector", c.s_v)
                p.wait("vector", c.s_ws)
                p.op("vector", lambda e: e.tensor_add(qbuf, bf_a, fchunk),
                     inc=c.s_v)
                p.wait("vector", c.s_v)
                p.op("vector", lambda e: e.tensor_scalar(
                    qbuf, qbuf, 1.0, -1.0, ALU.min, ALU.max), inc=c.s_v)
                p.wait("gpsimd", c.s_v)
                p.op("gpsimd", lambda e, q=qdst, k=k, qq=qq: e.dma_start(
                    out=q[k * 128:(k + 1) * 128, qq * 1024:(qq + 1) * 1024],
                    in_=qbuf), inc=c.s_ws, n=16)

    for t in range(NT):
        xt = c.X_TM[:, t * E:(t + 1) * E]
        p.wait("vector", c.s_ws)
        p.op("vector", lambda e, xt=xt: e.reduce_max(
            red2, xt, AX.X, apply_absolute_value=True), inc=c.s_v)
        p.wait("vector", c.s_v)
        p.op("vector", lambda e, t=t: e.tensor_scalar(
            tx_all[:, t:t + 1], red2, 1e-5, None, ALU.max), inc=c.s_v)
        p.wait("vector", c.s_v)
        p.op("vector", lambda e, t=t: e.reciprocal(sa_col, tx_all[:, t:t + 1]),
             inc=c.s_v)
        p.wait("vector", c.s_v)
        p.op("vector", lambda e: e.tensor_scalar(
            sa_col, sa_col, 127.0, None, ALU.mult), inc=c.s_v)
        p.wait("vector", c.s_v)
        p.op("vector", lambda e, xt=xt: e.tensor_scalar(
            c.AI16[:], xt, sa_col, -0.5, ALU.mult, ALU.add), inc=c.s_v)
        p.wait("vector", c.s_v)
        p.op("vector", lambda e: e.tensor_copy(bf_a, c.AI16[:]), inc=c.s_v)
        p.wait("vector", c.s_v)
        p.op("vector", lambda e, xt=xt: e.scalar_tensor_tensor(
            qbuf, xt, sa_col, bf_a, ALU.mult, ALU.subtract), inc=c.s_v)
        p.wait("vector", c.s_v)
        p.op("vector", lambda e: e.tensor_single_scalar(
            qbuf, qbuf, 0.5, ALU.is_ge), inc=c.s_v)
        p.wait("vector", c.s_v)
        p.op("vector", lambda e: e.tensor_add(qbuf, bf_a, qbuf), inc=c.s_v)
        p.wait("sync", c.s_v)
        for cc in range(NC):
            p.op("sync", lambda e, t=t, cc=cc: e.dma_start(
                out=c.X_FM[:, cc * L + t * 128:cc * L + (t + 1) * 128],
                in_=qbuf[:, cc * 128:(cc + 1) * 128],
                transpose=True), inc=c.s_tr, n=16)
        p.wait("vector", c.s_tr)

    NSUP = NT // 4
    p.wait("tensor", c.s_tr)
    mm_done = []
    for s in range(NSUP):
        t0 = 4 * s
        for oc in range(NC):
            for side, (qw_d, base, mb) in enumerate(
                    ((c.qgate_d, 0, mb_g), (c.qup_d, 4, mb_u))):
                for k in range(NC):
                    par = len(mm_done) % 2
                    wb = wbuf[par]
                    if len(mm_done) >= 2:
                        p.wait("gpsimd", c.s_t, mm_done[-2])
                    p.op("gpsimd", lambda e, q=qw_d, k=k, oc=oc, wb=wb:
                         e.dma_start(out=wb,
                                     in_=q[k * 128:(k + 1) * 128,
                                           oc * 512:(oc + 1) * 512]),
                         inc=c.s_wp[par], n=16)
                    p.wait("tensor", c.s_wp[par])
                    def mms(e, k=k, base=base, t0=t0, wb=wb):
                        li = None
                        for j in range(4):
                            tt = t0 + j
                            li = e.matmul(
                                c.PS[base + j],
                                c.X_FM[:, k * L + tt * 128:
                                       k * L + (tt + 1) * 128],
                                wb, start=(k == 0), stop=(k == NC - 1))
                        return li
                    p.op("tensor", mms, inc=c.s_t)
                    mm_done.append(c.s_t.n)
            for j in range(4):
                tt = t0 + j
                p.wait("vector", c.s_sc)
                p.wait("vector", c.s_v)
                p.op("vector", lambda e, tt=tt: e.tensor_scalar(
                    dq_col, tx_all[:, tt:tt + 1], mb_g, 1.0 / 127,
                    ALU.mult, ALU.mult), inc=c.s_v)
                p.wait("scalar", c.s_v)
                p.wait("scalar", c.s_t)
                p.op("scalar", lambda e, j=j: e.activation(
                    rgb[:, j * 512:(j + 1) * 512], c.PS[j],
                    ACT.Relu, scale=dq_col), inc=c.s_sc)
                p.wait("vector", c.s_sc)
                p.op("vector", lambda e, tt=tt: e.tensor_scalar(
                    dq_col, tx_all[:, tt:tt + 1], mb_u, 1.0 / 127,
                    ALU.mult, ALU.mult), inc=c.s_v)
                p.wait("vector", c.s_v)
                rg = rgb[:, j * 512:(j + 1) * 512]
                hd = hB[:, j * 4096 + oc * 512:j * 4096 + (oc + 1) * 512]
                p.op("vector", lambda e, j=j, rg=rg: e.tensor_tensor(
                    t1buf, c.PS[4 + j], rg, ALU.mult), inc=c.s_v)
                p.wait("vector", c.s_v)
                p.op("vector", lambda e, rg=rg, hd=hd: e.scalar_tensor_tensor(
                    hd, t1buf, dq_col, rg, ALU.mult, ALU.mult), inc=c.s_v)
            p.wait("tensor", c.s_v)
            p.wait("tensor", c.s_sc)
        for j in range(4):
            hrow = hB[:, j * 4096:(j + 1) * 4096]
            p.op("vector", lambda e, hrow=hrow: e.reduce_max(
                red2, hrow, AX.X, apply_absolute_value=True), inc=c.s_v)
            p.wait("vector", c.s_v)
            p.op("vector", lambda e: e.tensor_scalar(
                th_col, red2, 1e-5, None, ALU.max), inc=c.s_v)
            p.wait("vector", c.s_v)
            p.op("vector", lambda e: e.reciprocal(sa_col, th_col), inc=c.s_v)
            p.wait("vector", c.s_v)
            p.op("vector", lambda e: e.tensor_scalar(
                sa_col, sa_col, 127.0, None, ALU.mult), inc=c.s_v)
            p.wait("vector", c.s_v)
            p.op("vector", lambda e, j=j: e.tensor_scalar(
                c.ROW[:, 18 + j:19 + j], th_col, mb_d, 1.0 / 127,
                ALU.mult, ALU.mult), inc=c.s_v)
            for qq in range(4):
                hq = hB[:, j * 4096 + qq * 1024:j * 4096 + (qq + 1) * 1024]
                p.wait("vector", c.s_v)
                p.op("vector", lambda e, hq=hq: e.tensor_scalar(
                    c.AI16[:], hq, sa_col, -0.5, ALU.mult, ALU.add),
                    inc=c.s_v)
                p.wait("vector", c.s_v)
                p.op("vector", lambda e: e.tensor_copy(bf_a, c.AI16[:]),
                     inc=c.s_v)
                p.wait("vector", c.s_v)
                p.op("vector", lambda e, hq=hq: e.scalar_tensor_tensor(
                    hq, hq, sa_col, bf_a, ALU.mult, ALU.subtract), inc=c.s_v)
                p.wait("vector", c.s_v)
                p.op("vector", lambda e, hq=hq: e.tensor_single_scalar(
                    hq, hq, 0.5, ALU.is_ge), inc=c.s_v)
                p.wait("vector", c.s_v)
                p.op("vector", lambda e, hq=hq: e.tensor_add(hq, bf_a, hq),
                     inc=c.s_v)
        qhfm = [rgb[:, 0:512], rgb[:, 512:1024]]
        dwb = [(wbuf[0], wbuf[1]), (wbuf[2], wbuf[3])]
        p.wait("sync", c.s_v)
        p.wait("sync", c.s_t)
        p.wait("gpsimd", c.s_t)
        dmm_done = []
        for k in range(32):
            sl = k % 2
            if len(dmm_done) >= 2:
                p.wait("sync", c.s_t, dmm_done[-2])
                p.wait("gpsimd", c.s_t, dmm_done[-2])
            for j in range(4):
                p.op("sync", lambda e, k=k, j=j, sl=sl: e.dma_start(
                    out=qhfm[sl][:, j * 128:(j + 1) * 128],
                    in_=hB[:, j * 4096 + k * 128:j * 4096 + (k + 1) * 128],
                    transpose=True), inc=c.s_trp[sl], n=16)
            for half in range(2):
                p.op("gpsimd", lambda e, k=k, half=half, sl=sl: e.dma_start(
                    out=dwb[sl][half],
                    in_=c.qdown_d[k * 128:(k + 1) * 128,
                                  half * 512:(half + 1) * 512]),
                    inc=c.s_wp[sl], n=16)
            p.wait("tensor", c.s_wp[sl])
            p.wait("tensor", c.s_trp[sl])
            def dmms(e, k=k, sl=sl):
                li = None
                for j in range(4):
                    qk = qhfm[sl][:, j * 128:(j + 1) * 128]
                    for half in range(2):
                        li = e.matmul(c.PS[2 * j + half], qk,
                                      dwb[sl][half],
                                      start=(k == 0), stop=(k == 31))
                return li
            p.op("tensor", dmms, inc=c.s_t)
            dmm_done.append(c.s_t.n)
        mm_done.append(c.s_t.n)
        mm_done.append(c.s_t.n)
        for j in range(4):
            tt = t0 + j
            p.wait("vector", c.s_t)
            emit_ln(c, tt, c.PS[2 * j], c.PS[2 * j + 1], last,
                    scale_ap=c.ROW[:, 18 + j:19 + j])
        p.wait("tensor", c.s_v)
        p.wait("tensor", c.s_sc)


def _host_prep(inputs):
    f32 = np.float32
    perm64 = np.concatenate([np.arange(0, 64, 2), np.arange(1, 64, 2)])
    qk_perm = np.concatenate([64 * h + perm64 for h in range(H)])

    d = {}
    w_qkv0 = np.asarray(inputs["L0_attn_w_qkv"])
    rows = np.concatenate([qk_perm, 1024 + qk_perm, np.arange(2048, 3072)])
    d["w_qkv0T"] = w_qkv0[rows, :].T
    d["w_out0T"] = np.asarray(inputs["L0_attn_w_out"]).T
    signs = np.asarray(inputs["L0_attn_sorf_signs"])
    sig_t = signs.transpose(3, 0, 1, 2).reshape(64, 192).copy()
    s1 = signs[:, :, 0, :][:, :, perm64]
    for h in range(H):
        for r in range(4):
            sig_t[:, h * 12 + r * 3 + 0] = s1[h, r, :]
    d["sorf_signs_t"] = sig_t
    d["sorf_scales_t"] = np.asarray(
        inputs["L0_attn_sorf_scales"]).transpose(2, 0, 1).reshape(64, 64)
    d["sorf_scales_r"] = np.asarray(
        inputs["L0_attn_sorf_scales"]).reshape(64, 64)
    d["theta0"] = inputs["L0_ffn_theta"]
    d["phi0"] = inputs["L0_ffn_phi"]
    d["theta0n"] = inputs["L0_ffn_theta_noise"]
    d["phi0n"] = inputs["L0_ffn_phi_noise"]
    d["w_qkv1T"] = np.asarray(inputs["L1_attn_W_qkv"]).T
    b1 = np.asarray(inputs["L1_attn_b_qkv"])
    d["b_qkv1"] = b1.reshape(1, -1)
    d["b_qkv1c"] = b1.reshape(24, 128).T
    d["b_lat2"] = np.tile(np.asarray(inputs["L1_attn_b_lat"]), 2).reshape(128, 1)
    d["w_latT"] = np.asarray(inputs["L1_attn_W_lat"]).T
    d["b_lat"] = np.asarray(inputs["L1_attn_b_lat"]).reshape(1, -1)
    d["w_proj1T"] = np.asarray(inputs["L1_attn_W_proj"]).T
    d["b_proj1"] = np.asarray(inputs["L1_attn_b_proj"]).reshape(1, -1)
    d["w_gateT"] = np.asarray(inputs["L1_ffn_w_gate"]).T
    d["w_upT"] = np.asarray(inputs["L1_ffn_w_up"]).T
    d["w_downT"] = np.asarray(inputs["L1_ffn_w_down"]).T

    pos = np.maximum(np.arange(L) - 1, 0).astype(f32)
    d["posc"] = pos.reshape(NT, 128).T
    hm = np.array([[1.0]], dtype=f32)
    while hm.shape[0] < 64:
        hm = np.block([[hm, hm], [hm, -hm]])
    d["had_c"] = hm
    d["had_p"] = hm[:, perm64]
    d["grid_c"] = np.arange(32, dtype=f32).reshape(1, 32)
    d["masks_c"] = _build_masks()
    d["tenk_c"] = np.array([[10000.0]], dtype=f32)
    return {k: np.ascontiguousarray(v, dtype=np.float32) for k, v in d.items()}


def _build_masks():
    m = np.zeros((128, 144), dtype=np.float64)
    for i in range(128):
        m[i, i:i + 7] = 1.0
    return m


_NC_CACHE = {}


def kernel(**inputs):
    if "prog" not in _NC_CACHE:
        _NC_CACHE["prog"] = build_nc()
    nc = _NC_CACHE["prog"]
    shared = _host_prep(inputs)
    x = np.asarray(inputs["L0_attn_x"])
    in_maps = []
    for core in range(8):
        m = dict(shared)
        m["x"] = np.ascontiguousarray(x[core], dtype=np.float32)
        in_maps.append(m)
    res = run_bass_kernel_spmd(nc, in_maps, core_ids=list(range(8)))
    return np.stack([np.asarray(r["out"]) for r in res.results], axis=0)


# version 03
# speedup vs baseline: 1.0006x; 1.0006x over previous
import math
import numpy as np
import ml_dtypes
import concourse.bass as bass
import concourse.mybir as mybir
from concourse.bass_utils import run_bass_kernel_spmd
from contextlib import ExitStack

F32 = mybir.dt.float32
BF16 = mybir.dt.bfloat16
I32 = mybir.dt.int32
ALU = mybir.AluOpType
ACT = mybir.ActivationFunctionType
AX = mybir.AxisListType

L = 4096
E = 1024
NT = L // 128
NC = E // 128
H = 16
D = 64
MF = 256
I_FFN = 4096

STAGES = ("l0attn", "l0ffn", "l1attn", "l1ffn")
DEV_TAP = False
RACE_CHECK = False
L1A_LIMIT = None
L1A_P1PARTS = 3
L1A_PREP = 7
L1A_SUB = 9
L1A_EVENONLY = False

TWO_PI = np.float64(2 * np.pi)
CW1 = float(np.float32(TWO_PI))
CW2 = float(np.float32(TWO_PI - np.float64(np.float32(TWO_PI))))
CW3 = float(TWO_PI - np.float64(np.float32(CW1)) - np.float64(np.float32(CW2)))
INV2PI = float(np.float32(1.0 / TWO_PI))
PIH_HI = float(np.float32(np.pi / 2))
PIH_LO = float(np.float64(np.pi / 2) - np.float64(np.float32(np.pi / 2)))


class Sem:
    def __init__(self, stack, nc, name):
        self.h = stack.enter_context(nc.semaphore(name))
        self.n = 0


class Prog:
    ENGINES = ("sync", "gpsimd", "vector", "scalar", "tensor")

    def __init__(self):
        self.steps = {e: [] for e in self.ENGINES}

    def on(self, eng, fn):
        self.steps[eng].append(fn)

    def wait(self, eng, sem, val=None):
        v = sem.n if val is None else val
        self.on(eng, lambda e, s=sem.h, v=v: e.wait_ge(s, v))

    def op(self, eng, emit, inc=None, n=1):
        if inc is not None:
            inc.n += n
            tgt = (inc.h, n)
        else:
            tgt = None

        def f(e, emit=emit, tgt=tgt):
            i = emit(e)
            if tgt is not None:
                i.then_inc(*tgt)
        self.on(eng, f)


class Ctx:
    pass


def build_nc():
    nc = bass.Bass("TRN2", target_bir_lowering=False,
                   detect_race_conditions=RACE_CHECK)
    st = ExitStack()
    p = Prog()
    c = Ctx()
    c.nc, c.st, c.p = nc, st, p

    c.x_in = nc.dram_tensor("x", [L, E], F32, kind="ExternalInput").ap()
    c.out_d = nc.dram_tensor("out", [L, E], F32, kind="ExternalOutput").ap()
    if DEV_TAP:
        c.dbg_d = nc.dram_tensor("dbg", [L, E], F32, kind="ExternalOutput").ap()
        c.dbg2_d = nc.dram_tensor("dbg2", [L, 16], F32, kind="ExternalOutput").ap()

    def din(name, shape):
        return nc.dram_tensor(name, list(shape), F32, kind="ExternalInput").ap()

    c.w_qkv0T = din("w_qkv0T", [E, 3 * E])
    c.w_out0T = din("w_out0T", [E, E])
    c.sorf_signs_t = din("sorf_signs_t", [64, 192])
    c.sorf_scales_t = din("sorf_scales_t", [64, 64])
    c.sorf_scales_r = din("sorf_scales_r", [64, 64])
    c.th0 = din("theta0", [E, E])
    c.ph0 = din("phi0", [E, E])
    c.thn0 = din("theta0n", [E, E])
    c.phn0 = din("phi0n", [E, E])
    c.w_qkv1T = din("w_qkv1T", [E, 3 * E])
    c.b_qkv1 = din("b_qkv1", [1, 3 * E])
    c.w_latT = din("w_latT", [D, D])
    c.b_lat = din("b_lat", [1, D])
    c.w_proj1T = din("w_proj1T", [E, E])
    c.b_proj1 = din("b_proj1", [1, E])
    c.w_gateT = din("w_gateT", [E, I_FFN])
    c.w_upT = din("w_upT", [E, I_FFN])
    c.w_downT = din("w_downT", [I_FFN, E])
    c.posc = din("posc", [128, NT])
    c.had_c = din("had_c", [64, 64])
    c.had_p = din("had_p", [64, 64])
    c.grid_c = din("grid_c", [1, 32])
    c.fhi_c = din("fhi_c", [1, 32])
    c.masks_c = din("masks_c", [128, 144])
    c.b_qkv1c = din("b_qkv1c", [128, 24])
    c.b_lat2 = din("b_lat2", [128, 1])
    c.ten_thousand = din("tenk_c", [1, 1])

    c.qr_d = nc.dram_tensor("qr_d", [128, NC * L], BF16, kind="Internal").ap()
    c.kr_d = nc.dram_tensor("kr_d", [128, NC * L], BF16, kind="Internal").ap()
    c.qf_d = nc.dram_tensor("qf_d", [128, NC * L], BF16, kind="Internal").ap()
    c.kl_d = nc.dram_tensor("kl_d", [128, NC * L], BF16, kind="Internal").ap()
    c.vl_d = nc.dram_tensor("vl_d", [L, E], BF16, kind="Internal").ap()
    c.qgate_d = nc.dram_tensor("qgate_d", [E, I_FFN], BF16, kind="Internal").ap()
    c.qup_d = nc.dram_tensor("qup_d", [E, I_FFN], BF16, kind="Internal").ap()
    c.qdown_d = nc.dram_tensor("qdown_d", [I_FFN, E], BF16, kind="Internal").ap()

    def sb(name, shape, dt=F32):
        return st.enter_context(nc.sbuf_tensor(name, list(shape), dt))

    c.X_TM = sb("X_TM", [128, NT * E], BF16)
    c.X_FM = sb("X_FM", [128, NC * L], BF16)
    c.C_all = sb("C_all", [128, NT * 32], BF16)
    c.S_all = sb("S_all", [128, NT * 32], BF16)
    c.POS = sb("POS", [128, NT], F32)
    c.FREQB = sb("FREQB", [128, 32], F32)
    c.IDM = sb("IDM", [128, 128], BF16)
    c.ONES1 = sb("ONES1", [1, 512], BF16)
    c.NEGH = sb("NEGH", [128, 256], BF16)
    c.WS = sb("WS", [128, 8 * MF], BF16)
    c.ROW = sb("ROW", [128, 128], F32)
    c.ONESF_COL_T = sb("ONESF_COL_T", [128, 1], F32)
    c.ONESF_ROW_T = sb("ONESF_ROW_T", [1, 128], F32)
    c.ONESF_COL = c.ONESF_COL_T[:]
    c.ONESF_ROW = c.ONESF_ROW_T[:]
    c.ABF = sb("ABF", [128, 20480], BF16)
    c.ONESB_ROW = c.ABF[0:1, 18560:18688]
    c.BPROJR = c.ABF[0:1, 18688:19712]
    c.MASKB = c.ABF[:, 19712:19856]
    c.AF32 = sb("AF32", [128, 6656], F32)
    c.AI16 = sb("AI16", [128, 1024], mybir.dt.int16)
    c.OUTST = c.AF32[:, 1024:2048]

    c.PSA = st.enter_context(nc.psum_tensor("psA", [128, 2048], F32))
    c.PSB2 = st.enter_context(nc.psum_tensor("psB", [128, 2048], F32))
    c.PS = ([c.PSA[:, i * 512:(i + 1) * 512] for i in range(4)]
            + [c.PSB2[:, i * 512:(i + 1) * 512] for i in range(4)])
    c.PS0_11 = c.PSA[0:1, 0:1]
    c.PS0_C1 = c.PSA[:, 0:1]

    c.s_w = Sem(st, nc, "s_w")
    c.s_ldx = Sem(st, nc, "s_ldx")
    c.s_ws = Sem(st, nc, "s_ws")
    c.s_wp = [Sem(st, nc, "s_wp0"), Sem(st, nc, "s_wp1")]
    c.s_trp = [Sem(st, nc, "s_trp0"), Sem(st, nc, "s_trp1")]
    c.s_tr = Sem(st, nc, "s_tr")
    c.s_v = Sem(st, nc, "s_v")
    c.s_sc = Sem(st, nc, "s_sc")
    c.s_t = Sem(st, nc, "s_t")
    c.s_st = Sem(st, nc, "s_st")
    c.s_w2 = Sem(st, nc, "s_w2")
    c.s_ws2 = Sem(st, nc, "s_ws2")

    build_program(c)

    with nc.Block() as block:
        @block.sync
        def _(e):
            for f in p.steps["sync"]:
                f(e)

        @block.gpsimd
        def _(e):
            for f in p.steps["gpsimd"]:
                f(e)

        @block.vector
        def _(e):
            for f in p.steps["vector"]:
                f(e)

        @block.scalar
        def _(e):
            for f in p.steps["scalar"]:
                f(e)

        @block.tensor
        def _(e):
            for f in p.steps["tensor"]:
                f(e)

    return nc


def build_program(c):
    p = c.p
    p.op("vector", lambda e: e.memset(c.ONESF_COL, 1.0), inc=c.s_v)
    p.op("vector", lambda e: e.memset(c.ONESF_ROW, 1.0), inc=c.s_v)

    for t in range(NT):
        p.op("gpsimd",
             lambda e, t=t: e.dma_start(
                 out=c.X_TM[:, t * E:(t + 1) * E],
                 in_=c.x_in[t * 128:(t + 1) * 128, :]),
             inc=c.s_ldx, n=16)

    p.wait("sync", c.s_ldx)
    for t in range(NT):
        for cc in range(NC):
            p.op("sync",
                 lambda e, t=t, cc=cc: e.dma_start(
                     out=c.X_FM[:, cc * L + t * 128:cc * L + (t + 1) * 128],
                     in_=c.X_TM[:, t * E + cc * 128:t * E + (cc + 1) * 128],
                     transpose=True),
                 inc=c.s_tr, n=16)

    for eng in ("vector", "scalar", "tensor", "gpsimd"):
        p.wait(eng, c.s_ldx)
        p.wait(eng, c.s_tr)

    stages = list(STAGES)
    for si, stname in enumerate(stages):
        last = (si == len(stages) - 1)
        dict(l0ffn=emit_l0ffn, l0attn=emit_l0attn,
             l1attn=emit_l1attn, l1ffn=emit_l1ffn)[stname](c, last)

    p.wait("sync", c.s_st)


def emit_ln(c, t, ps_lo, ps_hi, last, scale_ap=None):
    p = c.p
    r = c.AF32[:, 0:1024]
    sq = c.AF32[:, 1024:2048]
    sums, ssq, mean, var, rstd, nbias, m2 = (c.ROW[:, i:i + 1] for i in range(7))

    op0 = ALU.add if scale_ap is None else ALU.mult
    sc0 = 0.0 if scale_ap is None else scale_ap
    p.wait("vector", c.s_sc)
    p.wait("vector", c.s_st)
    p.op("vector", lambda e: e.scalar_tensor_tensor(
        r[:, 0:512], ps_lo[:], sc0, c.X_TM[:, t * E:t * E + 512],
        op0, ALU.add, accum_out=sums), inc=c.s_v)
    p.op("vector", lambda e: e.scalar_tensor_tensor(
        r[:, 512:1024], ps_hi[:], sc0, c.X_TM[:, t * E + 512:(t + 1) * E],
        op0, ALU.add, accum_out=ssq), inc=c.s_v)
    p.wait("vector", c.s_v)
    p.op("vector", lambda e: e.tensor_add(sums, sums, ssq), inc=c.s_v)
    p.wait("scalar", c.s_v)
    if last:
        p.wait("scalar", c.s_st)
    p.op("scalar", lambda e: e.activation(sq, r, ACT.Square, accum_out=ssq),
         inc=c.s_sc)
    p.wait("vector", c.s_sc)
    p.op("vector", lambda e: e.tensor_scalar(mean, sums, 1.0 / E, None, ALU.mult),
         inc=c.s_v)
    p.op("vector", lambda e: e.tensor_scalar(var, ssq, 1.0 / E, 1e-5, ALU.mult,
                                             ALU.add), inc=c.s_v)
    p.wait("vector", c.s_v)
    p.op("vector", lambda e: e.tensor_mul(m2, mean, mean), inc=c.s_v)
    p.wait("vector", c.s_v)
    p.op("vector", lambda e: e.tensor_sub(var, var, m2), inc=c.s_v)
    p.wait("vector", c.s_v)
    p.op("vector", lambda e: e.reciprocal(var, var), inc=c.s_v)
    p.wait("scalar", c.s_v)
    p.op("scalar", lambda e: e.activation(rstd, var, ACT.Sqrt), inc=c.s_sc)
    p.wait("vector", c.s_sc)
    p.op("vector", lambda e: e.scalar_tensor_tensor(
        nbias, mean, -1.0, rstd, ALU.mult, ALU.mult), inc=c.s_v)
    p.wait("scalar", c.s_v)
    if last:
        p.wait("scalar", c.s_st)
        p.op("scalar", lambda e: e.activation(
            c.OUTST, r, ACT.Identity, bias=nbias, scale=rstd), inc=c.s_sc)
        p.wait("sync", c.s_sc)
        p.op("sync", lambda e: e.dma_start(
            out=c.out_d[t * 128:(t + 1) * 128, :], in_=c.OUTST),
            inc=c.s_st, n=16)
    else:
        p.op("scalar", lambda e: e.activation(
            c.X_TM[:, t * E:(t + 1) * E], r, ACT.Identity, bias=nbias, scale=rstd),
            inc=c.s_sc)
        p.wait("sync", c.s_sc)
        for cc in range(NC):
            p.op("sync", lambda e, cc=cc: e.dma_start(
                out=c.X_FM[:, cc * L + t * 128:cc * L + (t + 1) * 128],
                in_=c.X_TM[:, t * E + cc * 128:t * E + (cc + 1) * 128],
                transpose=True), inc=c.s_tr, n=16)


def emit_l0ffn(c, last):
    p = c.p
    Cw = c.ABF[:, 0:NC * E]
    Sw = c.ABF[:, NC * E:2 * NC * E]
    tbuf = c.AF32[:, 2048:3072]
    nbuf = c.AF32[:, 3072:4096]
    re2 = c.AF32[:, 2048:3072]
    im2 = c.AF32[:, 3072:4096]
    mag = c.AF32[:, 4096:5120]

    for cc in range(NC):
        for (wsrc, nsrc, dst, shift) in ((c.th0, c.thn0, Cw, True),
                                         (c.ph0, c.phn0, Sw, False)):
            p.wait("gpsimd", c.s_sc)
            p.op("gpsimd", lambda e, w=wsrc, cc=cc: e.dma_start(
                out=tbuf, in_=w[cc * 128:(cc + 1) * 128, :]), inc=c.s_w, n=16)
            p.op("gpsimd", lambda e, w=nsrc, cc=cc: e.dma_start(
                out=nbuf, in_=w[cc * 128:(cc + 1) * 128, :]), inc=c.s_w, n=16)
            p.wait("vector", c.s_w)
            if shift:
                p.op("vector", lambda e: e.tensor_add(tbuf, tbuf, nbuf),
                     inc=c.s_v)
                p.wait("vector", c.s_v)
                p.op("vector", lambda e: e.tensor_scalar(
                    tbuf, tbuf, PIH_HI, PIH_LO, ALU.add, ALU.add), inc=c.s_v)
            else:
                p.op("vector", lambda e: e.tensor_add(tbuf, tbuf, nbuf),
                     inc=c.s_v)
            p.wait("scalar", c.s_v)
            p.op("scalar", lambda e, d=dst, cc=cc: e.activation(
                d[:, cc * E:(cc + 1) * E], tbuf, ACT.Sin), inc=c.s_sc)

    onescol_b = c.ABF[:, 19856:19857]
    S_row = c.ABF[0:1, 19858:19986]
    NEGC = c.ABF[0:1, 16384:18432]
    p.wait("tensor", c.s_sc)
    p.wait("tensor", c.s_tr)
    p.op("vector", lambda e: e.memset(onescol_b, 1.0), inc=c.s_v)
    p.wait("tensor", c.s_v)

    def negc_mms(e):
        last_i = None
        for g, (W, half) in enumerate(((Cw, 0), (Cw, 1), (Sw, 0), (Sw, 1))):
            for cc in range(NC):
                last_i = e.matmul(
                    c.PS[4 + g][0:1, 0:512], onescol_b,
                    W[:, cc * E + half * 512:cc * E + (half + 1) * 512],
                    start=(cc == 0), stop=(cc == NC - 1))
        return last_i
    p.op("tensor", negc_mms, inc=c.s_t)
    p.wait("scalar", c.s_t)
    for g in range(4):
        p.op("scalar", lambda e, g=g: e.activation(
            NEGC[:, g * 512:(g + 1) * 512], c.PS[4 + g][0:1, 0:512],
            ACT.Identity, scale=-1.0 / 1024.0), inc=c.s_sc)
    p.wait("tensor", c.s_sc)

    for t in range(NT):
        def srow_mms(e, t=t):
            last_i = None
            for cc in range(NC):
                last_i = e.matmul(
                    c.PS[4][0:1, 0:128], onescol_b,
                    c.X_FM[:, cc * L + t * 128:cc * L + (t + 1) * 128],
                    start=(cc == 0), stop=(cc == NC - 1))
            return last_i
        p.op("tensor", srow_mms, inc=c.s_t)
        p.wait("scalar", c.s_t)
        p.op("scalar", lambda e: e.activation(
            S_row, c.PS[4][0:1, 0:128], ACT.Identity), inc=c.s_sc)
        p.wait("tensor", c.s_sc)

        def mm_block(e, t=t):
            last_i = None
            for base, W in ((0, Cw), (2, Sw)):
                for half in range(2):
                    for cc in range(NC):
                        e.matmul(
                            c.PS[base + half],
                            c.X_FM[:, cc * L + t * 128:cc * L + (t + 1) * 128],
                            W[:, cc * E + half * 512:cc * E + (half + 1) * 512],
                            start=(cc == 0), stop=False)
            for g in range(4):
                last_i = e.matmul(
                    c.PS[g], S_row, NEGC[:, g * 512:(g + 1) * 512],
                    start=False, stop=True)
            return last_i
        p.op("tensor", mm_block, inc=c.s_t)
        p.wait("scalar", c.s_t)
        for hh in range(2):
            p.op("scalar", lambda e, hh=hh: e.activation(
                re2[:, hh * 512:(hh + 1) * 512], c.PS[hh], ACT.Square),
                inc=c.s_sc)
            p.op("scalar", lambda e, hh=hh: e.activation(
                im2[:, hh * 512:(hh + 1) * 512], c.PS[2 + hh], ACT.Square),
                inc=c.s_sc)
        p.wait("vector", c.s_sc)
        p.op("vector", lambda e: e.tensor_add(re2, re2, im2), inc=c.s_v)
        p.wait("scalar", c.s_v)
        if DEV_TAP:
            p.wait("scalar", c.s_st)
        p.op("scalar", lambda e: e.activation(mag, re2, ACT.Sqrt), inc=c.s_sc)
        p.wait("vector", c.s_sc)
        emit_ln(c, t, mag[:, 0:512], mag[:, 512:1024], last)
        if DEV_TAP:
            p.wait("sync", c.s_v)
            p.wait("sync", c.s_sc)
            p.op("sync", lambda e, t=t: e.dma_start(
                out=c.dbg_d[t * 128:(t + 1) * 128, 0:1024],
                in_=c.AF32[:, 0:1024]), inc=c.s_st, n=16)
            p.op("sync", lambda e, t=t: e.dma_start(
                out=c.dbg2_d[t * 128:(t + 1) * 128, :],
                in_=c.ROW[:, 0:16]), inc=c.s_st, n=16)
            p.wait("vector", c.s_st)
            p.wait("scalar", c.s_st)
        p.wait("tensor", c.s_v)
        p.wait("tensor", c.s_sc)


def emit_l0attn(c, last):
    p = c.p
    sv = c.s_v
    WS = c.ABF[:, 0:4096]
    v65 = c.ABF[:, 4096:8256]
    stag1 = c.ABF[:, 8256:9280]
    stag2 = c.ABF[:, 9280:10304]
    stag3 = c.ABF[:, 10304:11328]
    rot_hm = c.ABF[:, 11328:15424]
    ksq = c.ABF[:, 15424:15936]
    kphi_tm = c.ABF[:, 15936:16192]
    wbuf = [c.ABF[:, 15424:15936], c.ABF[:, 15936:16448]]
    SpT = c.ABF[:, 16448:16704]
    S2 = c.ABF[:, 16768:19328]
    qphi = c.ABF[:, 19328:19584]
    oh_fm = c.ABF[:, 4096:8192]
    oh_tm = stag1
    hadb = c.ABF[:, 19584:19648]
    hadp_b = c.ABF[:, 19648:19712]
    scr_b = c.ABF[:, 19712:19776]
    hs = c.ABF[:, 19776:19840]
    asb = c.ABF[:, 19840:19904]
    c4096r = c.ABF[0:1, 19904:19968]
    frow = c.ABF[0:1, 19968:20000]

    S_acc = c.AF32[:, 2048:6144]
    sg_sb = c.AF32[:, 6144:6336]
    tgen = [c.AF32[:, 6336 + i * 32:6336 + (i + 1) * 32] for i in range(6)]
    ghb = c.AF32[:, 6528:6592]
    t_w = c.AF32[:, 6592:6656]
    NLN16 = -float(np.float32(np.log(16.0)))

    p.op("vector", lambda e: e.memset(v65, 1.0), inc=sv)
    p.op("vector", lambda e: e.memset(c4096r, 1.0 / 4096), inc=sv)
    p.op("vector", lambda e: e.memset(c.NEGH[:], -0.5), inc=sv)
    p.op("vector", lambda e: e.memset(c.ONESB_ROW, 1.0), inc=sv)
    p.op("vector", lambda e: e.memset(S_acc, 0.0), inc=sv)
    p.op("vector", lambda e: e.memset(c.ROW[:, 124:125], NLN16), inc=sv)
    p.wait("gpsimd", sv)
    p.op("gpsimd", lambda e: e.dma_start(out=sg_sb[0:64, :],
                                         in_=c.sorf_signs_t), inc=c.s_w, n=16)
    p.op("gpsimd", lambda e: e.dma_start(out=hadb[0:64, :], in_=c.had_c),
         inc=c.s_w, n=16)
    p.op("gpsimd", lambda e: e.dma_start(out=hadp_b[0:64, :], in_=c.had_p),
         inc=c.s_w, n=16)
    p.op("gpsimd", lambda e: e.dma_start(out=c.POS[:], in_=c.posc),
         inc=c.s_w, n=16)
    p.op("gpsimd", lambda e: e.dma_start(out=c.FREQB[0:1, :], in_=c.grid_c),
         inc=c.s_w, n=16)
    p.op("gpsimd", lambda e: e.dma_start(out=c.ROW[0:1, 0:32], in_=c.fhi_c),
         inc=c.s_w, n=16)

    fdel_row = c.ROW[0:1, 32:64]
    FDEL = c.PSA[:, 32:64]
    p.wait("scalar", c.s_w)
    p.op("scalar", lambda e: e.activation(
        c.FREQB[0:1, :], c.FREQB[0:1, :], ACT.Exp,
        scale=-math.log(10000.0) / 32.0), inc=c.s_sc)
    p.wait("vector", c.s_sc)
    p.op("vector", lambda e: e.tensor_sub(fdel_row, c.FREQB[0:1, :],
                                          c.ROW[0:1, 0:32]), inc=sv)
    p.wait("tensor", sv)
    p.op("tensor", lambda e: e.matmul(c.PSA[:, 0:32], c.ONESF_ROW,
                                      c.ROW[0:1, 0:32],
                                      start=True, stop=True), inc=c.s_t)
    p.op("tensor", lambda e: e.matmul(FDEL, c.ONESF_ROW, fdel_row,
                                      start=True, stop=True), inc=c.s_t)
    p.wait("vector", c.s_t)
    p.op("vector", lambda e: e.tensor_copy(c.FREQB[:], c.PSA[:, 0:32]),
         inc=sv)
    p.wait("vector", sv)
    for t in range(NT):
        ang, u, mf, r1, r2 = tgen[0], tgen[1], tgen[2], tgen[3], tgen[4]
        p.wait("vector", c.s_sc)
        p.op("vector", lambda e, t=t: e.tensor_scalar(
            ang, c.FREQB[:], c.POS[:, t:t + 1], None, ALU.mult), inc=sv)
        p.wait("vector", sv)
        p.op("vector", lambda e, t=t: e.tensor_scalar(
            tgen[5], FDEL, c.POS[:, t:t + 1], None, ALU.mult), inc=sv)
        p.wait("vector", sv)
        p.op("vector", lambda e: e.tensor_add(ang, ang, tgen[5]), inc=sv)
        p.wait("vector", sv)
        p.op("vector", lambda e: e.tensor_scalar(u, ang, INV2PI, None,
                                                 ALU.mult), inc=sv)
        p.wait("vector", sv)
        p.op("vector", lambda e: e.tensor_scalar(
            c.AI16[:, 0:32], u, 1.0, -0.5, ALU.mult, ALU.add), inc=sv)
        p.wait("vector", sv)
        p.op("vector", lambda e: e.tensor_copy(mf, c.AI16[:, 0:32]), inc=sv)
        p.wait("vector", sv)
        p.op("vector", lambda e: e.tensor_sub(tgen[5], u, mf), inc=sv)
        p.wait("vector", sv)
        p.op("vector", lambda e: e.tensor_single_scalar(
            u, tgen[5], 0.5, ALU.is_ge), inc=sv)
        p.wait("vector", sv)
        p.op("vector", lambda e: e.tensor_add(mf, mf, u), inc=sv)
        p.wait("vector", sv)
        p.op("vector", lambda e: e.tensor_single_scalar(
            u, tgen[5], 1.5, ALU.is_ge), inc=sv)
        p.wait("vector", sv)
        p.op("vector", lambda e: e.tensor_add(mf, mf, u), inc=sv)
        p.wait("vector", sv)
        p.op("vector", lambda e: e.tensor_single_scalar(
            u, tgen[5], -0.5, ALU.is_le), inc=sv)
        p.wait("vector", sv)
        p.op("vector", lambda e: e.tensor_sub(mf, mf, u), inc=sv)
        p.wait("vector", sv)
        p.op("vector", lambda e: e.scalar_tensor_tensor(
            r1, mf, -CW1, ang, ALU.mult, ALU.add), inc=sv)
        p.wait("vector", sv)
        p.op("vector", lambda e: e.scalar_tensor_tensor(
            r1, mf, -CW2, r1, ALU.mult, ALU.add), inc=sv)
        p.wait("vector", sv)
        p.op("vector", lambda e: e.scalar_tensor_tensor(
            r1, mf, -CW3, r1, ALU.mult, ALU.add), inc=sv)
        p.op("vector", lambda e: e.tensor_scalar(u, ang, INV2PI, 0.25,
                                                 ALU.mult, ALU.add), inc=sv)
        p.wait("vector", sv)
        p.op("vector", lambda e: e.tensor_scalar(
            c.AI16[:, 32:64], u, 1.0, -0.5, ALU.mult, ALU.add), inc=sv)
        p.wait("vector", sv)
        p.op("vector", lambda e: e.tensor_copy(mf, c.AI16[:, 32:64]), inc=sv)
        p.wait("vector", sv)
        p.op("vector", lambda e: e.tensor_sub(tgen[5], u, mf), inc=sv)
        p.wait("vector", sv)
        p.op("vector", lambda e: e.tensor_single_scalar(
            u, tgen[5], 0.5, ALU.is_ge), inc=sv)
        p.wait("vector", sv)
        p.op("vector", lambda e: e.tensor_add(mf, mf, u), inc=sv)
        p.wait("vector", sv)
        p.op("vector", lambda e: e.tensor_single_scalar(
            u, tgen[5], 1.5, ALU.is_ge), inc=sv)
        p.wait("vector", sv)
        p.op("vector", lambda e: e.tensor_add(mf, mf, u), inc=sv)
        p.wait("vector", sv)
        p.op("vector", lambda e: e.tensor_single_scalar(
            u, tgen[5], -0.5, ALU.is_le), inc=sv)
        p.wait("vector", sv)
        p.op("vector", lambda e: e.tensor_sub(mf, mf, u), inc=sv)
        p.wait("vector", sv)
        p.op("vector", lambda e: e.scalar_tensor_tensor(
            r2, mf, -CW1, ang, ALU.mult, ALU.add), inc=sv)
        p.wait("vector", sv)
        p.op("vector", lambda e: e.scalar_tensor_tensor(
            r2, mf, -CW2, r2, ALU.mult, ALU.add), inc=sv)
        p.wait("vector", sv)
        p.op("vector", lambda e: e.scalar_tensor_tensor(
            r2, mf, -CW3, r2, ALU.mult, ALU.add), inc=sv)
        p.wait("vector", sv)
        p.op("vector", lambda e: e.tensor_scalar(
            r2, r2, PIH_HI, PIH_LO, ALU.add, ALU.add), inc=sv)
        p.wait("scalar", sv)
        p.op("scalar", lambda e, t=t: e.activation(
            c.S_all[:, t * 32:(t + 1) * 32], r1, ACT.Sin), inc=c.s_sc)
        p.op("scalar", lambda e, t=t: e.activation(
            c.C_all[:, t * 32:(t + 1) * 32], r2, ACT.Sin), inc=c.s_sc)
    p.wait("vector", c.s_sc)
    p.op("vector", lambda e: e.memset(c.C_all[0:1, 0:32], 1.0), inc=sv)
    p.op("vector", lambda e: e.memset(c.S_all[0:1, 0:32], 0.0), inc=sv)

    p.wait("vector", c.s_w)
    for h in range(H):
        for r in range(4):
            col = h * 12 + r * 3
            p.wait("vector", c.s_t)
            p.op("vector", lambda e, col=col: e.tensor_scalar(
                hs[0:64, :], hadb[0:64, :], sg_sb[0:64, col + 2:col + 3],
                None, ALU.mult), inc=sv)
            p.wait("tensor", sv)
            p.op("tensor", lambda e: e.matmul(
                c.PSA[0:64, 0:64], hs[0:64, :], hadb[0:64, :],
                start=True, stop=True), inc=c.s_t)
            p.wait("vector", c.s_t)
            p.op("vector", lambda e: e.tensor_copy(asb[0:64, :],
                                                   c.PSA[0:64, 0:64]), inc=sv)
            p.wait("vector", sv)
            p.op("vector", lambda e, col=col: e.tensor_scalar(
                hs[0:64, :], hadp_b[0:64, :], sg_sb[0:64, col + 1:col + 2],
                None, ALU.mult), inc=sv)
            p.wait("tensor", sv)
            p.op("tensor", lambda e: e.matmul(
                c.PSA[0:64, 64:128], hs[0:64, :], asb[0:64, :],
                start=True, stop=True), inc=c.s_t)
            p.wait("gpsimd", c.s_t)
            p.op("gpsimd", lambda e, h=h, r=r: e.dma_start(
                out=scr_b[0:1, 0:64],
                in_=c.sorf_scales_r[h * 4 + r:h * 4 + r + 1, :]),
                inc=c.s_w, n=16)
            p.wait("tensor", c.s_w)
            p.op("tensor", lambda e: e.matmul(
                c.PSA[0:64, 128:192], c4096r, scr_b[0:1, 0:64],
                start=True, stop=True), inc=c.s_t)
            p.wait("vector", c.s_t)
            p.op("vector", lambda e: e.tensor_copy(ghb[0:64, :],
                                                   c.PSA[0:64, 128:192]),
                 inc=sv)
            p.op("vector", lambda e, col=col: e.tensor_scalar(
                t_w[0:64, :], c.PSA[0:64, 64:128],
                sg_sb[0:64, col + 0:col + 1], None, ALU.mult), inc=sv)
            p.wait("vector", sv)
            p.op("vector", lambda e, h=h, r=r: e.tensor_mul(
                WS[0:64, h * 256 + r * 64:h * 256 + (r + 1) * 64],
                t_w[0:64, :], ghb[0:64, :]), inc=sv)
            p.wait("tensor", sv)

    krv = c.kr_d.rearrange("p (cc t) -> p cc t", cc=NC)
    qrv = c.qr_d.rearrange("p (cc t) -> p cc t", cc=NC)
    rotv = rot_hm.rearrange("p (h t) -> p h t", h=16)
    v65v = v65.rearrange("p (s w) -> p s w", s=4)
    S2v = S2.rearrange("p (h j w) -> p h j w", h=16, j=2)
    CSbc = lambda T, tt: (T[:, tt * 32:(tt + 1) * 32]
                          .unsqueeze(1).broadcast_to([128, 16, 32]))
    mmld = []

    def rope_and_bounce(tt, rd):
        sh = lambda buf: buf.rearrange("p (h two q) -> p h two q", h=16, two=2)
        ev, od = sh(stag1)[:, :, 0, :], sh(stag1)[:, :, 1, :]
        rev, rod = sh(stag2)[:, :, 0, :], sh(stag2)[:, :, 1, :]
        tmp1 = stag3[:, 0:512].rearrange("p (h q) -> p h q", h=16)
        tmp2 = stag3[:, 512:1024].rearrange("p (h q) -> p h q", h=16)
        Cb, Sb = CSbc(c.C_all, tt), CSbc(c.S_all, tt)
        p.op("vector", lambda e: e.tensor_tensor(rev, ev, Cb, ALU.mult),
             inc=sv)
        p.op("vector", lambda e: e.tensor_tensor(tmp1, od, Sb, ALU.mult),
             inc=sv)
        p.op("vector", lambda e: e.tensor_tensor(tmp2, ev, Sb, ALU.mult),
             inc=sv)
        p.op("vector", lambda e: e.tensor_tensor(rod, od, Cb, ALU.mult),
             inc=sv)
        p.wait("vector", sv)
        p.op("vector", lambda e: e.tensor_sub(rev, rev, tmp1), inc=sv)
        p.op("vector", lambda e: e.tensor_add(rod, rod, tmp2), inc=sv)
        p.wait("sync", sv)
        for cc in range(NC):
            p.op("sync", lambda e, cc=cc: e.dma_start(
                out=stag3[:, cc * 128:(cc + 1) * 128],
                in_=stag2[:, cc * 128:(cc + 1) * 128], transpose=True),
                inc=c.s_tr, n=16)
        p.wait("sync", c.s_tr)
        p.op("sync", lambda e, tt=tt, rd=rd: e.dma_start(
            out=rd.rearrange("p (cc t) -> p cc t", cc=NC)[
                :, :, tt * 128:(tt + 1) * 128],
            in_=stag3.rearrange("p (cc t) -> p cc t", cc=NC)),
            inc=c.s_ws2, n=16)

    def produce_tm(T0, woff, kind):
        p.wait("gpsimd", c.s_t)
        for half in range(2):
            for cc in range(NC):
                par = len(mmld) % 2
                if len(mmld) >= 2:
                    p.wait("gpsimd", c.s_t, mmld[-2])
                p.op("gpsimd", lambda e, woff=woff, half=half, cc=cc,
                     par=par: e.dma_start(
                         out=wbuf[par],
                         in_=c.w_qkv0T[cc * 128:(cc + 1) * 128,
                                       woff + half * 512:
                                       woff + (half + 1) * 512]),
                     inc=c.s_wp[par], n=16)
                p.wait("tensor", c.s_wp[par])
                def mmf(e, half=half, cc=cc, T0=T0, par=par):
                    li = None
                    for s_ in range(4):
                        tt = T0 // 128 + s_
                        li = e.matmul(
                            c.PS[s_ + 4 * half],
                            c.X_FM[:, cc * L + tt * 128:
                                   cc * L + (tt + 1) * 128],
                            wbuf[par],
                            start=(cc == 0), stop=(cc == NC - 1))
                    return li
                p.op("tensor", mmf, inc=c.s_t)
                mmld.append(c.s_t.n)
        for s_ in range(4):
            tt = T0 // 128 + s_
            if kind == "v":
                p.wait("scalar", c.s_t)
                for half in range(2):
                    p.op("scalar", lambda e, s_=s_, half=half: e.activation(
                        v65v[:, s_, half * 520:half * 520 + 520]
                        .rearrange("p (h w) -> p h w", h=8)[:, :, 0:64],
                        c.PS[s_ + 4 * half], ACT.Identity), inc=c.s_sc)
            else:
                p.wait("scalar", c.s_t)
                p.wait("scalar", c.s_ws2)
                p.wait("scalar", sv)
                p.wait("scalar", c.s_tr)
                for half in range(2):
                    p.op("scalar", lambda e, s_=s_, half=half: e.activation(
                        stag1[:, half * 512:(half + 1) * 512],
                        c.PS[s_ + 4 * half], ACT.Identity), inc=c.s_sc)
                p.wait("vector", c.s_sc)
                p.wait("vector", c.s_tr)
                rope_and_bounce(tt, c.kr_d if kind == "k" else c.qr_d)
        p.wait("tensor", c.s_sc)
        if kind != "v":
            p.wait("tensor", sv)

    for t in range(NT // 4):
        T0 = t * 512
        produce_tm(T0, E, "k")
        produce_tm(T0, 2 * E, "v")
        p.wait("gpsimd", c.s_ws2)
        for pair in range(2):
            Tp = T0 + pair * 256
            p.wait("gpsimd", c.s_t)
            for hp in range(2):
                p.op("gpsimd", lambda e, Tp=Tp, hp=hp: e.dma_start(
                    out=rotv[0:64, hp::2, :],
                    in_=krv[hp * 64:hp * 64 + 64, :, Tp:Tp + 256]),
                    inc=c.s_w, n=16)
            p.wait("vector", c.s_w)
            for h in range(H):
                p.wait("vector", c.s_t)
                p.op("vector", lambda e, h=h: e.tensor_tensor(
                    ksq[0:64, 0:256], rotv[0:64, h, :],
                    rotv[0:64, h, :], ALU.mult), inc=sv)
                p.wait("tensor", sv)
                for sp in range(2):
                    s_ = pair * 2 + sp
                    p.wait("tensor", c.s_sc)
                    def phim(e, h=h, sp=sp):
                        e.matmul(c.PSB2[:, 0:256],
                                 rotv[0:64, h, sp * 128:(sp + 1) * 128],
                                 WS[0:64, h * 256:(h + 1) * 256],
                                 start=True, stop=False)
                        return e.matmul(
                            c.PSB2[:, 0:256],
                            ksq[0:64, sp * 128:(sp + 1) * 128],
                            c.NEGH[0:64, 0:256],
                            start=False, stop=True)
                    p.op("tensor", phim, inc=c.s_t)
                    p.wait("scalar", c.s_t)
                    p.op("scalar", lambda e: e.activation(
                        kphi_tm, c.PSB2[:, 0:256], ACT.Exp,
                        bias=c.ROW[:, 124:125]), inc=c.s_sc)
                    p.wait("tensor", c.s_sc)
                    def smm2(e, h=h, s_=s_, sp=sp):
                        return e.matmul(
                            c.PSB2[0:65, 512:768],
                            v65v[:, s_, h * 65:h * 65 + 65],
                            kphi_tm,
                            start=(sp == 0), stop=(sp == 1))
                    p.op("tensor", smm2, inc=c.s_t)
                p.wait("vector", c.s_t)
                p.op("vector", lambda e, h=h: e.tensor_add(
                    S_acc[0:65, h * 256:(h + 1) * 256],
                    S_acc[0:65, h * 256:(h + 1) * 256],
                    c.PSB2[0:65, 512:768]), inc=sv)
                p.wait("tensor", sv)

    p.wait("vector", c.s_t)
    p.op("vector", lambda e: e.memset(SpT[:], 0.0), inc=sv)
    p.wait("vector", sv)
    for h in range(H):
        p.wait("vector", sv)
        p.op("vector", lambda e, h=h: e.tensor_copy(
            SpT[0:65, :], S_acc[0:65, h * 256:(h + 1) * 256]), inc=sv)
        p.wait("sync", sv)
        p.wait("sync", c.s_t)
        for j in range(2):
            p.op("sync", lambda e, h=h, j=j: e.dma_start(
                out=S2v[:, h, j, 0:80],
                in_=SpT[0:80, j * 128:(j + 1) * 128], transpose=True),
                inc=c.s_tr, n=16)
        p.wait("vector", c.s_tr)

    nln16c = NLN16
    for t in range(NT // 4):
        T0 = t * 512
        produce_tm(T0, 0, "q")
        p.wait("gpsimd", c.s_ws2)
        for pair in range(2):
            Tp = T0 + pair * 256
            p.wait("gpsimd", c.s_t)
            for hp in range(2):
                p.op("gpsimd", lambda e, Tp=Tp, hp=hp: e.dma_start(
                    out=rotv[0:64, hp::2, :],
                    in_=qrv[hp * 64:hp * 64 + 64, :, Tp:Tp + 256]),
                    inc=c.s_w, n=16)
            p.wait("vector", c.s_w)
            for sp in range(2):
                s_ = pair * 2 + sp
                tt = T0 // 128 + s_
                for h in range(H):
                    p.wait("vector", c.s_t)
                    p.op("vector", lambda e, h=h, sp=sp: e.tensor_tensor(
                        ksq[0:64, 0:128], rotv[0:64, h, sp * 128:(sp + 1) * 128],
                        rotv[0:64, h, sp * 128:(sp + 1) * 128], ALU.mult),
                        inc=sv)
                    p.wait("tensor", sv)
                    p.wait("tensor", c.s_sc)
                    def qphim(e, h=h, sp=sp):
                        for j in range(2):
                            e.matmul(c.PSB2[:, 0:128 if j == 0 else 256][
                                :, j * 128:(j + 1) * 128] if False else
                                c.PSB2[:, j * 128:(j + 1) * 128],
                                WS[0:64, h * 256 + j * 128:
                                   h * 256 + (j + 1) * 128],
                                rotv[0:64, h, sp * 128:(sp + 1) * 128],
                                start=True, stop=False)
                            li = e.matmul(
                                c.PSB2[:, j * 128:(j + 1) * 128],
                                c.NEGH[0:64, 0:128],
                                ksq[0:64, 0:128],
                                start=False, stop=True)
                        return li
                    p.op("tensor", qphim, inc=c.s_t)
                    p.wait("scalar", c.s_t)
                    p.op("scalar", lambda e: e.activation(
                        qphi, c.PSB2[:, 0:256], ACT.Exp,
                        bias=c.ROW[:, 124:125]), inc=c.s_sc)
                    p.wait("tensor", c.s_sc)
                    def nmm(e, h=h):
                        for j in range(2):
                            li = e.matmul(
                                c.PSA[:, h * 128:h * 128 + 65],
                                qphi[:, j * 128:(j + 1) * 128],
                                S2v[:, h, j, 0:65],
                                start=(j == 0), stop=(j == 1))
                        return li
                    p.op("tensor", nmm, inc=c.s_t)
                p.wait("vector", c.s_t)
                nview = c.PSA.rearrange("p (h w) -> p h w", h=16)
                p.op("vector", lambda e: e.tensor_scalar(
                    c.ROW[:, 108:124], nview[:, :, 64], 1e-6, None, ALU.max),
                    inc=sv)
                p.wait("vector", sv)
                p.op("vector", lambda e: e.reciprocal(
                    c.ROW[:, 108:124], c.ROW[:, 108:124]), inc=sv)
                p.wait("vector", sv)
                p.wait("vector", c.s_tr)
                p.wait("vector", c.s_ws2)
                p.wait("vector", c.s_sc)
                p.op("vector", lambda e: e.tensor_tensor(
                    oh_tm.rearrange("p (h w) -> p h w", h=16),
                    nview[:, :, 0:64],
                    c.ROW[:, 108:124].unsqueeze(2).broadcast_to([128, 16, 64]),
                    ALU.mult), inc=sv)
                p.wait("sync", sv)
                p.wait("sync", c.s_t)
                for cc in range(NC):
                    p.op("sync", lambda e, s_=s_, cc=cc: e.dma_start(
                        out=oh_fm[:, cc * 512 + s_ * 128:
                                  cc * 512 + (s_ + 1) * 128],
                        in_=oh_tm[:, cc * 128:(cc + 1) * 128],
                        transpose=True), inc=c.s_tr, n=16)
                p.wait("tensor", sv)
        p.wait("tensor", c.s_tr)
        p.wait("gpsimd", c.s_t)
        prj = []
        for k in range(NC):
            for hf in range(2):
                par = len(prj) % 2
                if len(prj) >= 2:
                    p.wait("gpsimd", c.s_t, prj[-2])
                p.op("gpsimd", lambda e, k=k, par=par, hf=hf: e.dma_start(
                    out=wbuf[par],
                    in_=c.w_out0T[k * 128:(k + 1) * 128,
                                  hf * 512:(hf + 1) * 512]),
                    inc=c.s_wp[par], n=16)
                p.wait("tensor", c.s_wp[par])
                def pjm(e, k=k, par=par, hf=hf):
                    li = None
                    for s_ in range(4):
                        li = e.matmul(
                            c.PS[2 * s_ + hf],
                            oh_fm[:, k * 512 + s_ * 128:
                                  k * 512 + (s_ + 1) * 128],
                            wbuf[par],
                            start=(k == 0), stop=(k == NC - 1))
                    return li
                p.op("tensor", pjm, inc=c.s_t)
                prj.append(c.s_t.n)
        for s_ in range(4):
            p.wait("vector", c.s_t)
            emit_ln(c, 4 * t + s_, c.PS[2 * s_], c.PS[2 * s_ + 1], last)
        p.wait("tensor", sv)
        p.wait("tensor", c.s_sc)
        p.wait("sync", sv)


def emit_l1attn(c, last):
    p = c.p
    q_sb = c.ABF[:, 0:4096]
    kv_sb = c.ABF[:, 4096:8192]
    lat_sb = c.ABF[:, 8192:12288]
    vtm_sb = c.ABF[:, 12288:16384]
    wbuf = [c.ABF[:, 16384 + i * 512:16384 + (i + 1) * 512] for i in range(4)]
    W2 = c.ABF[:, 18432:18560]
    q2h = c.ABF[:, 0:2048]
    span = c.ABF[:, 2048:4224]
    vlat2 = c.ABF[:, 4224:5248]
    vedge = c.ABF[:, 5248:6272]
    Pst = c.ABF[:, 6272:7552]
    PT1 = c.ABF[:, 7552:7680]
    PT2 = c.ABF[:, 7680:7808]
    oh_tm = c.ABF[:, 7808:8832]
    kedgeL = c.ABF[:, 8832:8848]
    kedge0 = c.ABF[:, 8848:8864]
    oh_fm = c.ABF[:, 12288:16384]
    bq_c = c.AF32[:, 3072:3096]
    blat2 = c.AF32[:, 3096:3097]
    negmax = c.ROW[:, 8:16]
    sums8 = c.ROW[:, 64:80]
    recs = c.ROW[:, 80:96]

    p.op("vector", lambda e: e.memset(W2, 0.0), inc=c.s_v)
    p.op("vector", lambda e: e.memset(Pst, 0.0), inc=c.s_v)
    p.op("vector", lambda e: e.memset(vedge, 0.0), inc=c.s_v)
    p.op("vector", lambda e: e.memset(c.ONESB_ROW, 1.0), inc=c.s_v)
    p.wait("gpsimd", c.s_v)
    prep_ops = [
        lambda e: e.dma_start(out=c.ABF[0:64, 18432:18496], in_=c.w_latT),
        lambda e: e.dma_start(out=c.ABF[64:128, 18496:18560], in_=c.w_latT),
        lambda e: e.dma_start(out=bq_c, in_=c.b_qkv1c),
        lambda e: e.dma_start(out=blat2, in_=c.b_lat2),
        lambda e: e.dma_start(out=c.BPROJR, in_=c.b_proj1),
        lambda e: e.dma_start(out=c.MASKB, in_=c.masks_c),
    ]
    for fop in prep_ops[:L1A_PREP]:
        p.op("gpsimd", fop, inc=c.s_w, n=16)

    p.wait("tensor", c.s_w)
    p.wait("tensor", c.s_v)
    ld_done = []
    for t in range((NT // 4) if L1A_P1PARTS > 0 else 0):
        T0 = t * 512
        for part, (dst, bias_off) in enumerate(
                ((q_sb, 0), (kv_sb, 8), (kv_sb, 16))[:L1A_P1PARTS]):
            for oc in range(NC):
                for k in range(NC):
                    par = len(ld_done) % 2
                    wb = wbuf[par]
                    if len(ld_done) >= 2:
                        p.wait("gpsimd", c.s_t, ld_done[-2])
                    p.op("gpsimd", lambda e, k=k, oc=oc, part=part, wb=wb:
                         e.dma_start(out=wb[:, 0:128], in_=c.w_qkv1T[
                             k * 128:(k + 1) * 128,
                             part * E + oc * 128:part * E + (oc + 1) * 128]),
                         inc=c.s_wp[par], n=16)
                    p.wait("tensor", c.s_wp[par])
                    p.op("tensor", lambda e, k=k, oc=oc, T0=T0, wb=wb:
                         e.matmul(c.PS[oc % 4],
                                  wb[:, 0:128],
                                  c.X_FM[:, k * L + T0:k * L + T0 + 512],
                                  start=(k == 0), stop=(k == NC - 1)),
                         inc=c.s_t)
                    ld_done.append(c.s_t.n)
                if oc % 4 == 3:
                    p.wait("scalar", c.s_t)
                    for jj in range(4):
                        occ = oc - 3 + jj
                        p.op("scalar", lambda e, occ=occ, jj=jj, dst=dst,
                             bias_off=bias_off: e.activation(
                                 dst[:, occ * 512:(occ + 1) * 512],
                                 c.PS[jj], ACT.Identity,
                                 bias=bq_c[:, bias_off + occ:
                                           bias_off + occ + 1]), inc=c.s_sc)
                    p.wait("tensor", c.s_sc)
            if part == 0:
                p.wait("sync", c.s_sc)
                for cc in range(NC):
                    p.op("sync", lambda e, T0=T0, cc=cc: e.dma_start(
                        out=c.qf_d[:, cc * L + T0:cc * L + T0 + 512],
                        in_=q_sb[:, cc * 512:(cc + 1) * 512]),
                        inc=c.s_ws2, n=16)
            else:
                p.wait("scalar", c.s_ws2)
                p.wait("scalar", c.s_v)
                for cc in range(NC):
                    p.wait("tensor", c.s_sc)
                    p.op("tensor", lambda e, cc=cc: e.matmul(
                        c.PS[4 + cc % 2], W2,
                        kv_sb[:, cc * 512:(cc + 1) * 512],
                        start=True, stop=True), inc=c.s_t)
                    p.wait("scalar", c.s_t)
                    p.op("scalar", lambda e, cc=cc: e.activation(
                        lat_sb[:, cc * 512:(cc + 1) * 512],
                        c.PS[4 + cc % 2], ACT.Identity,
                        bias=blat2), inc=c.s_sc)
                p.wait("sync", c.s_sc)
                if part == 1:
                    for cc in range(NC):
                        p.op("sync", lambda e, T0=T0, cc=cc: e.dma_start(
                            out=c.kl_d[:, cc * L + T0:cc * L + T0 + 512],
                            in_=lat_sb[:, cc * 512:(cc + 1) * 512]),
                            inc=c.s_ws2, n=16)

                else:
                    for jj in range(4):
                        for cc in range(NC):
                            p.op("sync", lambda e, jj=jj, cc=cc: e.dma_start(
                                out=vtm_sb[:, jj * 1024 + cc * 128:
                                           jj * 1024 + (cc + 1) * 128],
                                in_=lat_sb[:, cc * 512 + jj * 128:
                                           cc * 512 + (jj + 1) * 128],
                                transpose=True), inc=c.s_tr, n=16)
                    p.wait("sync", c.s_tr)
                    for jj in range(4):
                        p.op("sync", lambda e, T0=T0, jj=jj: e.dma_start(
                            out=c.vl_d[T0 + jj * 128:T0 + (jj + 1) * 128, :],
                            in_=vtm_sb[:, jj * 1024:(jj + 1) * 1024]),
                            inc=c.s_ws2, n=16)
            p.wait("scalar", c.s_t)
        p.wait("scalar", c.s_ws2)
        p.wait("sync", c.s_ws2)

    p.wait("gpsimd", c.s_ws2)
    p.wait("vector", c.s_ws2)
    p.op("vector", lambda e: e.memset(Pst, 0.0), inc=c.s_v)
    p.op("vector", lambda e: e.memset(vedge, 0.0), inc=c.s_v)
    p.wait("gpsimd", c.s_v)
    p.wait("sync", c.s_v)
    qfv = c.qf_d.rearrange("p (c t) -> p c t", c=NC)
    klv0 = c.kl_d.rearrange("p (c t) -> p c t", c=NC)
    kevw = [kedgeL.rearrange("p (h x) -> p h x", h=16),
            kedge0.rearrange("p (h x) -> p h x", h=16)]
    for which, tok in ((0, L - 1), (1, 0)):
        for hp in range(2):
            p.op("gpsimd", lambda e, which=which, tok=tok, hp=hp: e.dma_start(
                out=kevw[which][0:64, hp::2, 0:1],
                in_=klv0[hp * 64:hp * 64 + 64, :, tok:tok + 1],
                allow_slow_non_contiguous=True), inc=c.s_w, n=16)
    klv = c.kl_d.rearrange("p (c t) -> p c t", c=NC)
    spanv = span.rearrange("p (h t) -> p h t", h=16)
    q2v = q2h.rearrange("p (h t) -> p h t", h=16)
    for j in range(NT if L1A_LIMIT is None else L1A_LIMIT):
        t0 = j * 128
        p.wait("gpsimd", c.s_t)
        p.wait("gpsimd", c.s_tr)
        for hp in range(2):
            p.op("gpsimd", lambda e, t0=t0, hp=hp: e.dma_start(
                out=q2v[0:64, hp::2, :], in_=qfv[hp * 64:hp * 64 + 64,
                                                :, t0:t0 + 128]),
                inc=c.s_w, n=16)
        lo = max(t0 - 3, 0)
        hi = min(t0 + 131, L)
        off = lo - (t0 - 3)
        wid = hi - lo
        for hp in range(2):
            p.op("gpsimd", lambda e, lo=lo, off=off, wid=wid, hp=hp:
                 e.dma_start(out=spanv[0:64, hp::2, off:off + wid],
                             in_=klv[hp * 64:hp * 64 + 64, :, lo:lo + wid]),
                 inc=c.s_w, n=16)
        need_left_fill = bool(off)
        need_right_fill = hi < t0 + 131
        if t0 >= 3:
            p.op("gpsimd", lambda e, t0=t0: e.dma_start(
                out=vlat2, in_=c.vl_d[t0 - 3:t0 + 125, :]), inc=c.s_w, n=16)
        else:
            for i in range(3):
                p.op("gpsimd", lambda e, i=i: e.dma_start(
                    out=vlat2.rearrange("p (a b) -> p a b", a=1)[
                        i:i + 1, 0, 0:1024] if False else
                    vlat2[i:i + 1, 0:1024], in_=c.vl_d[L - 1:L, :]),
                    inc=c.s_w, n=16)
            p.op("gpsimd", lambda e: e.dma_start(
                out=vlat2[3:128, 0:1024], in_=c.vl_d[0:125, :]),
                inc=c.s_w, n=16)
        if t0 + 131 <= L:
            p.op("gpsimd", lambda e, t0=t0: e.dma_start(
                out=vedge[0:6, 0:1024], in_=c.vl_d[t0 + 125:t0 + 131, :]),
                inc=c.s_w, n=16)
        else:
            p.op("gpsimd", lambda e, t0=t0: e.dma_start(
                out=vedge[0:3, 0:1024], in_=c.vl_d[t0 + 125:t0 + 128, :]),
                inc=c.s_w, n=16)
            for i in range(3):
                p.op("gpsimd", lambda e, i=i: e.dma_start(
                    out=vedge[3 + i:4 + i, 0:1024], in_=c.vl_d[0:1, :]),
                    inc=c.s_w, n=16)
        if need_left_fill or need_right_fill:
            p.wait("vector", c.s_w)
            if need_left_fill:
                p.op("vector", lambda e: e.tensor_copy(
                    spanv[0:64, :, 0:3],
                    kedgeL.rearrange("p (h x) -> p h x", h=16)[
                        0:64, :, 0:1].broadcast_to([64, 16, 3])), inc=c.s_v)
            if need_right_fill:
                p.op("vector", lambda e: e.tensor_copy(
                    spanv[0:64, :, 131:134],
                    kedge0.rearrange("p (h x) -> p h x", h=16)[
                        0:64, :, 0:1].broadcast_to([64, 16, 3])), inc=c.s_v)
        p.wait("tensor", c.s_w)
        p.wait("tensor", c.s_v)
        if L1A_SUB < 2:
            continue
        for g in range(2):
            p.wait("tensor", c.s_sc)
            p.wait("tensor", c.s_v)
            def smm(e, g=g):
                li = None
                for i in range(8):
                    h = g * 8 + i
                    li = e.matmul(
                        c.PSA[:, i * 256:i * 256 + 134],
                        q2v[0:64, h, :],
                        spanv[0:64, h, 0:134],
                        start=True, stop=True)
                return li
            p.op("tensor", smm, inc=c.s_t)
            if L1A_SUB < 3:
                continue
            p.wait("vector", c.s_t)
            p.op("vector", lambda e: e.tensor_reduce(
                negmax, c.PSA.rearrange("p (i w) -> p i w", i=8)[:, :, 0:134],
                AX.X, ALU.max, negate=True), inc=c.s_v)
            p.wait("vector", c.s_v)
            p.op("vector", lambda e: e.tensor_scalar(
                negmax, negmax, 0.125, None, ALU.mult), inc=c.s_v)
            p.wait("scalar", c.s_v)
            p.wait("scalar", c.s_tr)
            for i in range(8):
                p.op("scalar", lambda e, i=i: e.activation(
                    Pst[:, i * 144:i * 144 + 134],
                    c.PSA[:, i * 256:i * 256 + 134], ACT.Exp,
                    bias=negmax[:, i:i + 1], scale=0.125), inc=c.s_sc)
            p.wait("vector", c.s_sc)
            pstv = Pst[:, 0:1152].rearrange("p (i w) -> p i w", i=8)
            p.op("vector", lambda e: e.tensor_tensor(
                pstv[:, 0:8, 0:134], pstv[:, 0:8, 0:134],
                c.MASKB[:, 0:134].unsqueeze(1).broadcast_to([128, 8, 134]),
                ALU.mult), inc=c.s_v)
            p.wait("vector", c.s_v)
            p.op("vector", lambda e, g=g: e.tensor_reduce(
                sums8[:, g * 8:(g + 1) * 8], pstv[:, 0:8, 0:134],
                AX.X, ALU.add), inc=c.s_v)
            p.wait("sync", c.s_v)
            if L1A_SUB < 4:
                continue
            for i in range(8):
                h = g * 8 + i
                if not (g == 0 and i == 0):
                    p.wait("sync", c.s_t)
                p.op("sync", lambda e, i=i: e.dma_start(
                    out=PT1, in_=Pst[:, i * 144:i * 144 + 128],
                    transpose=True), inc=c.s_tr, n=16)
                p.op("sync", lambda e, i=i: e.dma_start(
                    out=PT2, in_=Pst[:, i * 144 + 128:i * 144 + 256],
                    transpose=True), inc=c.s_tr, n=16)
                p.wait("tensor", c.s_tr)
                def pv(e, h=h):
                    e.matmul(c.PSB2[:, h * 64:(h + 1) * 64], PT1,
                             vlat2[:, h * 64:(h + 1) * 64],
                             start=True, stop=False)
                    return e.matmul(c.PSB2[:, h * 64:(h + 1) * 64],
                                    PT2[0:8, :],
                                    vedge[0:8, h * 64:(h + 1) * 64],
                                    start=False, stop=True)
                p.op("tensor", pv, inc=c.s_t)
        if L1A_SUB < 5:
            continue
        p.wait("vector", c.s_t)
        p.op("vector", lambda e: e.reciprocal(recs, sums8), inc=c.s_v)
        p.wait("vector", c.s_v)
        p.op("vector", lambda e: e.tensor_tensor(
            oh_tm.rearrange("p (h d) -> p h d", h=16),
            c.PSB2[:, 0:1024].rearrange("p (h d) -> p h d", h=16),
            recs.unsqueeze(2).broadcast_to([128, 16, 64]), ALU.mult),
            inc=c.s_v)
        p.wait("sync", c.s_v)
        p.wait("sync", c.s_t)
        jj = j % 4
        for cc in range(NC):
            p.op("sync", lambda e, jj=jj, cc=cc: e.dma_start(
                out=oh_fm[:, cc * 512 + jj * 128:cc * 512 + (jj + 1) * 128],
                in_=oh_tm[:, cc * 128:(cc + 1) * 128], transpose=True),
                inc=c.s_tr, n=16)
        p.wait("tensor", c.s_v)
        if jj == 3:
            p.wait("tensor", c.s_tr)
            prj_done = []
            for k in range(NC):
                par = k % 2
                if len(prj_done) >= 2:
                    p.wait("gpsimd", c.s_t, prj_done[-2])
                p.op("gpsimd", lambda e, k=k, par=par: e.dma_start(
                    out=wbuf[2 * par], in_=c.w_proj1T[
                        k * 128:(k + 1) * 128, 0:512]),
                    inc=c.s_wp[par], n=16)
                p.op("gpsimd", lambda e, k=k, par=par: e.dma_start(
                    out=wbuf[2 * par + 1], in_=c.w_proj1T[
                        k * 128:(k + 1) * 128, 512:1024]),
                    inc=c.s_wp[par], n=16)
                p.wait("tensor", c.s_wp[par])
                def pmm(e, k=k, par=par):
                    li = None
                    for ts_ in range(4):
                        for hf in range(2):
                            li = e.matmul(
                                c.PS[2 * ts_ + hf],
                                oh_fm[:, k * 512 + ts_ * 128:
                                      k * 512 + (ts_ + 1) * 128],
                                wbuf[2 * par + hf],
                                start=(k == 0), stop=False)
                    return li
                p.op("tensor", pmm, inc=c.s_t)
                prj_done.append(c.s_t.n)
            def bias_mms(e):
                li = None
                for ts_ in range(4):
                    for hf in range(2):
                        li = e.matmul(c.PS[2 * ts_ + hf], c.ONESB_ROW,
                                      c.BPROJR[0:1, hf * 512:(hf + 1) * 512],
                                      start=False, stop=True)
                return li
            p.op("tensor", bias_mms, inc=c.s_t)
            for ts_ in range(4):
                p.wait("vector", c.s_t)
                emit_ln(c, j - 3 + ts_, c.PS[2 * ts_], c.PS[2 * ts_ + 1],
                        last)
            p.wait("tensor", c.s_v)
            p.wait("tensor", c.s_sc)
            p.wait("sync", c.s_v)


def emit_l1ffn(c, last):
    p = c.p
    hB = c.ABF[:, 0:16384]
    wbuf = [c.ABF[:, 16384 + i * 512:16384 + (i + 1) * 512] for i in range(4)]
    rgb = c.ABF[:, 18432:20480]
    qbuf = c.ABF[:, 18432:19456]
    bf_a = c.ABF[:, 19456:20480]
    fchunk = c.AF32[:, 2048:3072]
    t1buf = c.AF32[:, 4096:4608]

    wsum = c.ROW[:, 8:9]
    red1 = c.ROW[:, 9:10]
    mb_g, mb_u, mb_d = (c.ROW[:, i:i + 1] for i in (10, 11, 12))
    sw_col = c.ROW[:, 13:14]
    th_col = c.ROW[:, 14:15]
    dq_col = c.ROW[:, 15:16]
    sa_col = c.ROW[:, 16:17]
    tot_sb = c.ROW[:, 17:18]
    red2 = c.ROW[:, 22:23]
    tx_all = c.ROW[:, 32:64]

    for (wsrc, qdst, mb, kchunks) in ((c.w_gateT, c.qgate_d, mb_g, 8),
                                      (c.w_upT, c.qup_d, mb_u, 8),
                                      (c.w_downT, c.qdown_d, mb_d, 32)):
        ncol = wsrc.shape[1]
        quarters = ncol // 1024
        p.op("vector", lambda e: e.memset(wsum, 0.0), inc=c.s_v)
        for k in range(kchunks):
            for qq in range(quarters):
                p.wait("gpsimd", c.s_v)
                p.op("gpsimd", lambda e, w=wsrc, k=k, qq=qq: e.dma_start(
                    out=fchunk, in_=w[k * 128:(k + 1) * 128,
                                      qq * 1024:(qq + 1) * 1024]),
                    inc=c.s_w, n=16)
                p.wait("vector", c.s_w)
                p.op("vector", lambda e: e.reduce_sum(
                    red1, fchunk, AX.X, apply_absolute_value=True), inc=c.s_v)
                p.wait("vector", c.s_v)
                p.op("vector", lambda e: e.tensor_add(wsum, wsum, red1),
                     inc=c.s_v)
        p.wait("tensor", c.s_v)
        p.op("tensor", lambda e: e.matmul(c.PS0_11, c.ONESF_COL,
                                          wsum, start=True, stop=True),
             inc=c.s_t)
        p.wait("vector", c.s_t)
        p.op("vector", lambda e: e.tensor_copy(tot_sb[0:1, :],
                                               c.PS0_11), inc=c.s_v)
        p.wait("tensor", c.s_v)
        p.op("tensor", lambda e: e.matmul(c.PS0_C1, c.ONESF_ROW,
                                          tot_sb[0:1, :], start=True,
                                          stop=True), inc=c.s_t)
        p.wait("vector", c.s_t)
        p.op("vector", lambda e, n=float(wsrc.shape[0] * ncol): e.tensor_scalar(
            red2, c.PS0_C1, 1.0 / n, None, ALU.mult), inc=c.s_v)
        p.wait("vector", c.s_v)
        p.op("vector", lambda e, mb=mb: e.tensor_scalar(
            mb, red2, 1e-5, None, ALU.max), inc=c.s_v)
        p.wait("vector", c.s_v)
        p.op("vector", lambda e, mb=mb: e.reciprocal(sw_col, mb), inc=c.s_v)
        for k in range(kchunks):
            for qq in range(quarters):
                p.wait("gpsimd", c.s_v)
                p.op("gpsimd", lambda e, w=wsrc, k=k, qq=qq: e.dma_start(
                    out=fchunk, in_=w[k * 128:(k + 1) * 128,
                                      qq * 1024:(qq + 1) * 1024]),
                    inc=c.s_w, n=16)
                p.wait("vector", c.s_w)
                p.op("vector", lambda e: e.tensor_scalar(
                    c.AI16[:], fchunk, sw_col, -0.5, ALU.mult, ALU.add),
                    inc=c.s_v)
                p.wait("vector", c.s_v)
                p.op("vector", lambda e: e.tensor_copy(bf_a, c.AI16[:]),
                     inc=c.s_v)
                p.wait("vector", c.s_v)
                p.op("vector", lambda e: e.scalar_tensor_tensor(
                    fchunk, fchunk, sw_col, bf_a, ALU.mult, ALU.subtract),
                    inc=c.s_v)
                p.wait("vector", c.s_v)
                p.op("vector", lambda e: e.tensor_single_scalar(
                    fchunk, fchunk, 0.5, ALU.is_ge), inc=c.s_v)
                p.wait("vector", c.s_v)
                p.wait("vector", c.s_ws)
                p.op("vector", lambda e: e.tensor_add(qbuf, bf_a, fchunk),
                     inc=c.s_v)
                p.wait("vector", c.s_v)
                p.op("vector", lambda e: e.tensor_scalar(
                    qbuf, qbuf, 1.0, -1.0, ALU.min, ALU.max), inc=c.s_v)
                p.wait("gpsimd", c.s_v)
                p.op("gpsimd", lambda e, q=qdst, k=k, qq=qq: e.dma_start(
                    out=q[k * 128:(k + 1) * 128, qq * 1024:(qq + 1) * 1024],
                    in_=qbuf), inc=c.s_ws, n=16)

    for t in range(NT):
        xt = c.X_TM[:, t * E:(t + 1) * E]
        p.wait("vector", c.s_ws)
        p.op("vector", lambda e, xt=xt: e.reduce_max(
            red2, xt, AX.X, apply_absolute_value=True), inc=c.s_v)
        p.wait("vector", c.s_v)
        p.op("vector", lambda e, t=t: e.tensor_scalar(
            tx_all[:, t:t + 1], red2, 1e-5, None, ALU.max), inc=c.s_v)
        p.wait("vector", c.s_v)
        p.op("vector", lambda e, t=t: e.reciprocal(sa_col, tx_all[:, t:t + 1]),
             inc=c.s_v)
        p.wait("vector", c.s_v)
        p.op("vector", lambda e: e.tensor_scalar(
            sa_col, sa_col, 127.0, None, ALU.mult), inc=c.s_v)
        p.wait("vector", c.s_v)
        p.op("vector", lambda e, xt=xt: e.tensor_scalar(
            c.AI16[:], xt, sa_col, -0.5, ALU.mult, ALU.add), inc=c.s_v)
        p.wait("vector", c.s_v)
        p.op("vector", lambda e: e.tensor_copy(bf_a, c.AI16[:]), inc=c.s_v)
        p.wait("vector", c.s_v)
        p.op("vector", lambda e, xt=xt: e.scalar_tensor_tensor(
            qbuf, xt, sa_col, bf_a, ALU.mult, ALU.subtract), inc=c.s_v)
        p.wait("vector", c.s_v)
        p.op("vector", lambda e: e.tensor_single_scalar(
            qbuf, qbuf, 0.5, ALU.is_ge), inc=c.s_v)
        p.wait("vector", c.s_v)
        p.op("vector", lambda e: e.tensor_add(qbuf, bf_a, qbuf), inc=c.s_v)
        p.wait("sync", c.s_v)
        for cc in range(NC):
            p.op("sync", lambda e, t=t, cc=cc: e.dma_start(
                out=c.X_FM[:, cc * L + t * 128:cc * L + (t + 1) * 128],
                in_=qbuf[:, cc * 128:(cc + 1) * 128],
                transpose=True), inc=c.s_tr, n=16)
        p.wait("vector", c.s_tr)

    NSUP = NT // 4
    p.wait("tensor", c.s_tr)
    mm_done = []
    for s in range(NSUP):
        t0 = 4 * s
        for oc in range(NC):
            for side, (qw_d, base, mb) in enumerate(
                    ((c.qgate_d, 0, mb_g), (c.qup_d, 4, mb_u))):
                for k in range(NC):
                    par = len(mm_done) % 2
                    wb = wbuf[par]
                    if len(mm_done) >= 2:
                        p.wait("gpsimd", c.s_t, mm_done[-2])
                    p.op("gpsimd", lambda e, q=qw_d, k=k, oc=oc, wb=wb:
                         e.dma_start(out=wb,
                                     in_=q[k * 128:(k + 1) * 128,
                                           oc * 512:(oc + 1) * 512]),
                         inc=c.s_wp[par], n=16)
                    p.wait("tensor", c.s_wp[par])
                    def mms(e, k=k, base=base, t0=t0, wb=wb):
                        li = None
                        for j in range(4):
                            tt = t0 + j
                            li = e.matmul(
                                c.PS[base + j],
                                c.X_FM[:, k * L + tt * 128:
                                       k * L + (tt + 1) * 128],
                                wb, start=(k == 0), stop=(k == NC - 1))
                        return li
                    p.op("tensor", mms, inc=c.s_t)
                    mm_done.append(c.s_t.n)
            for j in range(4):
                tt = t0 + j
                p.wait("vector", c.s_sc)
                p.wait("vector", c.s_v)
                p.op("vector", lambda e, tt=tt: e.tensor_scalar(
                    dq_col, tx_all[:, tt:tt + 1], mb_g, 1.0 / 127,
                    ALU.mult, ALU.mult), inc=c.s_v)
                p.wait("scalar", c.s_v)
                p.wait("scalar", c.s_t)
                p.op("scalar", lambda e, j=j: e.activation(
                    rgb[:, j * 512:(j + 1) * 512], c.PS[j],
                    ACT.Relu, scale=dq_col), inc=c.s_sc)
                p.wait("vector", c.s_sc)
                p.op("vector", lambda e, tt=tt: e.tensor_scalar(
                    dq_col, tx_all[:, tt:tt + 1], mb_u, 1.0 / 127,
                    ALU.mult, ALU.mult), inc=c.s_v)
                p.wait("vector", c.s_v)
                rg = rgb[:, j * 512:(j + 1) * 512]
                hd = hB[:, j * 4096 + oc * 512:j * 4096 + (oc + 1) * 512]
                p.op("vector", lambda e, j=j, rg=rg: e.tensor_tensor(
                    t1buf, c.PS[4 + j], rg, ALU.mult), inc=c.s_v)
                p.wait("vector", c.s_v)
                p.op("vector", lambda e, rg=rg, hd=hd: e.scalar_tensor_tensor(
                    hd, t1buf, dq_col, rg, ALU.mult, ALU.mult), inc=c.s_v)
            p.wait("tensor", c.s_v)
            p.wait("tensor", c.s_sc)
        for j in range(4):
            hrow = hB[:, j * 4096:(j + 1) * 4096]
            p.op("vector", lambda e, hrow=hrow: e.reduce_max(
                red2, hrow, AX.X, apply_absolute_value=True), inc=c.s_v)
            p.wait("vector", c.s_v)
            p.op("vector", lambda e: e.tensor_scalar(
                th_col, red2, 1e-5, None, ALU.max), inc=c.s_v)
            p.wait("vector", c.s_v)
            p.op("vector", lambda e: e.reciprocal(sa_col, th_col), inc=c.s_v)
            p.wait("vector", c.s_v)
            p.op("vector", lambda e: e.tensor_scalar(
                sa_col, sa_col, 127.0, None, ALU.mult), inc=c.s_v)
            p.wait("vector", c.s_v)
            p.op("vector", lambda e, j=j: e.tensor_scalar(
                c.ROW[:, 18 + j:19 + j], th_col, mb_d, 1.0 / 127,
                ALU.mult, ALU.mult), inc=c.s_v)
            for qq in range(4):
                hq = hB[:, j * 4096 + qq * 1024:j * 4096 + (qq + 1) * 1024]
                p.wait("vector", c.s_v)
                p.op("vector", lambda e, hq=hq: e.tensor_scalar(
                    c.AI16[:], hq, sa_col, -0.5, ALU.mult, ALU.add),
                    inc=c.s_v)
                p.wait("vector", c.s_v)
                p.op("vector", lambda e: e.tensor_copy(bf_a, c.AI16[:]),
                     inc=c.s_v)
                p.wait("vector", c.s_v)
                p.op("vector", lambda e, hq=hq: e.scalar_tensor_tensor(
                    hq, hq, sa_col, bf_a, ALU.mult, ALU.subtract), inc=c.s_v)
                p.wait("vector", c.s_v)
                p.op("vector", lambda e, hq=hq: e.tensor_single_scalar(
                    hq, hq, 0.5, ALU.is_ge), inc=c.s_v)
                p.wait("vector", c.s_v)
                p.op("vector", lambda e, hq=hq: e.tensor_add(hq, bf_a, hq),
                     inc=c.s_v)
        qhfm = [rgb[:, 0:512], rgb[:, 512:1024]]
        dwb = [(wbuf[0], wbuf[1]), (wbuf[2], wbuf[3])]
        p.wait("sync", c.s_v)
        p.wait("sync", c.s_t)
        p.wait("gpsimd", c.s_t)
        dmm_done = []
        for k in range(32):
            sl = k % 2
            if len(dmm_done) >= 2:
                p.wait("sync", c.s_t, dmm_done[-2])
                p.wait("gpsimd", c.s_t, dmm_done[-2])
            for j in range(4):
                p.op("sync", lambda e, k=k, j=j, sl=sl: e.dma_start(
                    out=qhfm[sl][:, j * 128:(j + 1) * 128],
                    in_=hB[:, j * 4096 + k * 128:j * 4096 + (k + 1) * 128],
                    transpose=True), inc=c.s_trp[sl], n=16)
            for half in range(2):
                p.op("gpsimd", lambda e, k=k, half=half, sl=sl: e.dma_start(
                    out=dwb[sl][half],
                    in_=c.qdown_d[k * 128:(k + 1) * 128,
                                  half * 512:(half + 1) * 512]),
                    inc=c.s_wp[sl], n=16)
            p.wait("tensor", c.s_wp[sl])
            p.wait("tensor", c.s_trp[sl])
            def dmms(e, k=k, sl=sl):
                li = None
                for j in range(4):
                    qk = qhfm[sl][:, j * 128:(j + 1) * 128]
                    for half in range(2):
                        li = e.matmul(c.PS[2 * j + half], qk,
                                      dwb[sl][half],
                                      start=(k == 0), stop=(k == 31))
                return li
            p.op("tensor", dmms, inc=c.s_t)
            dmm_done.append(c.s_t.n)
        mm_done.append(c.s_t.n)
        mm_done.append(c.s_t.n)
        for j in range(4):
            tt = t0 + j
            p.wait("vector", c.s_t)
            emit_ln(c, tt, c.PS[2 * j], c.PS[2 * j + 1], last,
                    scale_ap=c.ROW[:, 18 + j:19 + j])
        p.wait("tensor", c.s_v)
        p.wait("tensor", c.s_sc)


def _host_prep(inputs):
    f32 = np.float32
    perm64 = np.concatenate([np.arange(0, 64, 2), np.arange(1, 64, 2)])
    qk_perm = np.concatenate([64 * h + perm64 for h in range(H)])

    d = {}
    w_qkv0 = np.asarray(inputs["L0_attn_w_qkv"])
    rows = np.concatenate([qk_perm, 1024 + qk_perm, np.arange(2048, 3072)])
    d["w_qkv0T"] = w_qkv0[rows, :].T
    d["w_out0T"] = np.asarray(inputs["L0_attn_w_out"]).T
    signs = np.asarray(inputs["L0_attn_sorf_signs"])
    sig_t = signs.transpose(3, 0, 1, 2).reshape(64, 192).copy()
    s1 = signs[:, :, 0, :][:, :, perm64]
    for h in range(H):
        for r in range(4):
            sig_t[:, h * 12 + r * 3 + 0] = s1[h, r, :]
    d["sorf_signs_t"] = sig_t
    d["sorf_scales_t"] = np.asarray(
        inputs["L0_attn_sorf_scales"]).transpose(2, 0, 1).reshape(64, 64)
    d["sorf_scales_r"] = np.asarray(
        inputs["L0_attn_sorf_scales"]).reshape(64, 64)
    d["theta0"] = inputs["L0_ffn_theta"]
    d["phi0"] = inputs["L0_ffn_phi"]
    d["theta0n"] = inputs["L0_ffn_theta_noise"]
    d["phi0n"] = inputs["L0_ffn_phi_noise"]
    d["w_qkv1T"] = np.asarray(inputs["L1_attn_W_qkv"]).T
    b1 = np.asarray(inputs["L1_attn_b_qkv"])
    d["b_qkv1"] = b1.reshape(1, -1)
    d["b_qkv1c"] = b1.reshape(24, 128).T
    d["b_lat2"] = np.tile(np.asarray(inputs["L1_attn_b_lat"]), 2).reshape(128, 1)
    d["w_latT"] = np.asarray(inputs["L1_attn_W_lat"]).T
    d["b_lat"] = np.asarray(inputs["L1_attn_b_lat"]).reshape(1, -1)
    d["w_proj1T"] = np.asarray(inputs["L1_attn_W_proj"]).T
    d["b_proj1"] = np.asarray(inputs["L1_attn_b_proj"]).reshape(1, -1)
    d["w_gateT"] = np.asarray(inputs["L1_ffn_w_gate"]).T
    d["w_upT"] = np.asarray(inputs["L1_ffn_w_up"]).T
    d["w_downT"] = np.asarray(inputs["L1_ffn_w_down"]).T

    pos = np.maximum(np.arange(L) - 1, 0).astype(f32)
    d["posc"] = pos.reshape(NT, 128).T
    hm = np.array([[1.0]], dtype=f32)
    while hm.shape[0] < 64:
        hm = np.block([[hm, hm], [hm, -hm]])
    d["had_c"] = hm
    d["had_p"] = hm[:, perm64]
    d["grid_c"] = np.arange(32, dtype=f32).reshape(1, 32)
    d["fhi_c"] = (10000.0 ** (-np.arange(32) / 32.0)).astype(f32).reshape(1, 32)
    d["masks_c"] = _build_masks()
    d["tenk_c"] = np.array([[10000.0]], dtype=f32)
    return {k: np.ascontiguousarray(v, dtype=np.float32) for k, v in d.items()}


def _build_masks():
    m = np.zeros((128, 144), dtype=np.float64)
    for i in range(128):
        m[i, i:i + 7] = 1.0
    return m


_NC_CACHE = {}


def kernel(**inputs):
    if "prog" not in _NC_CACHE:
        _NC_CACHE["prog"] = build_nc()
    nc = _NC_CACHE["prog"]
    shared = _host_prep(inputs)
    x = np.asarray(inputs["L0_attn_x"])
    in_maps = []
    for core in range(8):
        m = dict(shared)
        m["x"] = np.ascontiguousarray(x[core], dtype=np.float32)
        in_maps.append(m)
    res = run_bass_kernel_spmd(nc, in_maps, core_ids=list(range(8)))
    return np.stack([np.asarray(r["out"]) for r in res.results], axis=0)
